# version 16
# baseline (speedup 1.0000x reference)
import math
import numpy as np
import concourse.bass as bass
import concourse.mybir as mybir
from concourse.alu_op_type import AluOpType as OP
from concourse.bass_utils import run_bass_kernel_spmd

F32 = mybir.dt.float32
I32 = mybir.dt.int32
BF16 = mybir.dt.bfloat16
AF = mybir.ActivationFunctionType
AX = mybir.AxisListType

DIM = 1024
NH = 16
RMS_EPS = 1.1920929e-07
EPS_LIN = 1e-06
LN_EPS = 1e-05
THETA = 10000.0
ROUTED_SCALE = 2.5
NT_ALL = 32
NT_OWN = 16
NQ = 4
NT_Q = NT_OWN // NQ
NE = 10
NSLOT = 3 * NE

TWO_PI = 2.0 * math.pi
CW1 = 6.28125
CW2 = float(np.float32(TWO_PI - CW1))
CW3 = float(TWO_PI - CW1 - CW2)

ENGINES = ("sync", "gpsimd", "vector", "scalar", "tensor")


class Prog:

    def __init__(self, nc):
        self.nc = nc
        self.steps = {e: [] for e in ENGINES}
        self.sems = {}
        self.cnt = {}

    def sem(self, name):
        h = self.nc.alloc_semaphore(name)
        self.sems[name] = h
        self.cnt[name] = 0
        return h

    def on(self, eng, fn):
        self.steps[eng].append(fn)

    def wait(self, eng, sem_name, val=None):
        if val is None:
            val = self.cnt[sem_name]
        h = self.sems[sem_name]
        if val > 0:
            self.on(eng, lambda e, h=h, v=val: e.wait_ge(h, v))

    def inc(self, sem_name, amount=1):
        self.cnt[sem_name] += amount
        return self.cnt[sem_name]

    def tag(self, ins_fn, sem_name, amount=1):
        import traceback
        h = self.sems[sem_name]
        self.inc(sem_name, amount)
        site = "".join(traceback.format_stack(limit=6)[:-1])

        def f(e, ins_fn=ins_fn, h=h, amount=amount, site=site):
            ins = ins_fn(e)
            if ins is None:
                raise RuntimeError("op returned None; emitted at:\n" + site)
            ins.then_inc(h, amount)
        return f

    def replay(self):
        nc = self.nc
        with nc.Block() as block:
            @block.sync
            def _(e):
                for f in self.steps["sync"]:
                    f(e)

            @block.gpsimd
            def _(e):
                for f in self.steps["gpsimd"]:
                    f(e)

            @block.vector
            def _(e):
                for f in self.steps["vector"]:
                    f(e)

            @block.scalar
            def _(e):
                for f in self.steps["scalar"]:
                    f(e)

            @block.tensor
            def _(e):
                for f in self.steps["tensor"]:
                    f(e)


def build_program(taps=(), phases=3, race_safe=False):
    taps = set(taps)
    nc = bass.Bass("TRN2", target_bir_lowering=False)

    xb = nc.dram_tensor("xb", [NT_ALL * 128, DIM], F32, kind="ExternalInput").ap()
    wkvT = nc.dram_tensor("wkvT", [DIM, 2 * DIM], F32, kind="ExternalInput").ap()
    wqT = nc.dram_tensor("wqT", [DIM, DIM], F32, kind="ExternalInput").ap()
    woutT = nc.dram_tensor("woutT", [DIM, DIM], F32, kind="ExternalInput").ap()
    bqkv = nc.dram_tensor("bqkv", [1, 3 * DIM], F32, kind="ExternalInput").ap()
    gqv = nc.dram_tensor("gqv", [1, DIM], F32, kind="ExternalInput").ap()
    gkv = nc.dram_tensor("gkv", [1, DIM], F32, kind="ExternalInput").ap()
    goutv = nc.dram_tensor("goutv", [1, DIM], F32, kind="ExternalInput").ap()
    boutv = nc.dram_tensor("boutv", [1, DIM], F32, kind="ExternalInput").ap()
    gatew = nc.dram_tensor("gatew", [DIM, 8], F32, kind="ExternalInput").ap()
    gateb = nc.dram_tensor("gateb", [1, 8], F32, kind="ExternalInput").ap()
    wg = nc.dram_tensor("wg", [8, DIM, 512], F32, kind="ExternalInput").ap()
    wu = nc.dram_tensor("wu", [8, DIM, 512], F32, kind="ExternalInput").ap()
    wd = nc.dram_tensor("wd", [8, 512, DIM], F32, kind="ExternalInput").ap()
    swg = nc.dram_tensor("swg", [DIM, DIM], F32, kind="ExternalInput").ap()
    swu = nc.dram_tensor("swu", [DIM, DIM], F32, kind="ExternalInput").ap()
    swd = nc.dram_tensor("swd", [DIM, DIM], F32, kind="ExternalInput").ap()
    posoff = nc.dram_tensor("posoff", [1, NT_ALL], F32, kind="ExternalInput").ap()
    y = nc.dram_tensor("y", [NT_OWN * 128, DIM], F32, kind="ExternalOutput").ap()
    wsp = nc.dram_tensor("wsp", [NSLOT, 128, 8192], BF16, kind="Internal").ap()
    wspq = nc.dram_tensor("wspq", [4, 128, 8192], BF16, kind="Internal").ap()

    tap_t = {}
    def tap(name, shape):
        if name in taps:
            tap_t[name] = nc.dram_tensor("tap_" + name, shape, F32,
                                         kind="ExternalOutput").ap()
            return tap_t[name]
        return None

    t_vkt = tap("vkt", [128, NH * 65])
    t_kf0 = tap("kf0", [128, DIM])
    t_x2 = tap("x2", [NT_OWN * 128, DIM])
    t_wd = tap("wdense", [128, NQ * NT_Q * 8])
    t_qf0 = tap("qf0", [128, DIM])
    t_attn0 = tap("attn0", [128, DIM])
    t_y0 = tap("ffny0", [128, DIM])

    NAR = 53200
    AR = nc.alloc_sbuf_tensor("AR", [128, NAR], F32)
    ARP = AR.ap()

    def V(off, n):
        assert off >= 0 and off + n <= NAR, (off, n)
        return ARP[:, off:off + n]

    def VB(off, n_units):
        return V(off, n_units).bitcast(BF16)

    ident = V(0, 128)
    c_rmseps = V(128, 1)
    c_lneps = V(129, 1)
    ssq = V(130, 1)
    rstd = V(131, 1)
    smean = V(132, 1)
    svar = V(133, 1)
    st1 = V(134, 1)
    ssum = V(135, 1)
    WD = V(136, 32)
    wsum = V(168, 1)
    wnorm = V(169, 1)
    dnm = V(176, 16)
    rcp = V(192, 16)
    emask4 = V(216, 16)
    emask8 = V(232, 64)
    poffv = V(296, 32)

    X2_BASE = 512
    X2 = [V(X2_BASE + i * DIM, DIM) for i in range(NT_OWN)]

    MID = X2_BASE + NT_OWN * DIM
    cos_tab = V(MID, 1024)
    sin_tab = V(MID + 1024, 1024)
    VKTh = VB(MID + 2048, 520)
    VKTl = VB(MID + 2568, 520)
    posf = V(MID + 3088, 32)
    invf = V(MID + 3120, 32)
    B = MID + 3152
    assert NAR - B >= 33000

    a1_WKVh = VB(B, 8192)
    a1_WKVl = VB(B + 8192, 8192)
    a1_bkv = V(B + 16384, 2048)
    a1_gk = V(B + 18432, 1024)
    a1_xin = [V(B + 19456, 1024), V(B + 20480, 1024)]
    a1_xTh = VB(B + 21504, 512)
    a1_xTl = VB(B + 22016, 512)
    a1_kv = V(B + 22528, 2048)
    a1_kn = V(B + 24576, 1024)
    a1_kfh = [VB(B + 25600, 512), VB(B + 26624, 512)]
    a1_kfl = [VB(B + 26112, 512), VB(B + 27136, 512)]
    a1_vph = [VB(B + 27648, 520), VB(B + 28688, 520)]
    a1_vpl = [VB(B + 28168, 520), VB(B + 29208, 520)]
    a1_scr = V(B + 29728, 1024)
    assert B + 30752 <= NAR
    spw_a = V(B + 30752, 1024)
    spw_b = V(B + 31776, 1024)
    spw_hi = VB(B + 31776, 512)
    spw_lo = VB(B + 32288, 512)
    assert B + 32800 <= NAR

    a2_WQh = VB(B, 4096)
    a2_WQl = VB(B + 4096, 4096)
    a2_WOh = VB(B + 8192, 4096)
    a2_WOl = VB(B + 12288, 4096)
    a2_bq = V(B + 16384, 1024)
    a2_gq = V(B + 17408, 1024)
    a2_bout = V(B + 18432, 1024)
    a2_gout = V(B + 19456, 1024)
    a2_xin = [V(B + 20480, 1024), V(B + 21504, 1024)]
    a2_TSh = VB(B + 22528, 512)
    a2_TSl = VB(B + 23040, 512)
    a2_b1 = V(B + 23552, 1024)
    a2_b2 = V(B + 24576, 1024)
    a2_b3 = V(B + 25600, 1024)
    a2_b4 = V(B + 26624, 1024)
    a2_scr = V(B + 27648, 1024)
    sp_in = [V(B + 28672, 1024), V(B + 29696, 1024)]
    sp_out = [V(B + 30720, 1024), V(B + 31744, 1024)]
    assert B + 32768 <= NAR

    c_ring = [VB(B + s * 4096, 4096) for s in range(4)]
    c_xTqh = VB(B + 16384, 2048)
    c_xTql = VB(B + 18432, 2048)
    c_Yq = [V(B + 20480 + j * 1024, 1024) for j in range(NT_Q)]
    c_g = [V(B + 24576 + j * 512, 512) for j in range(NT_Q)]
    c_gu = [V(B + 26624 + j * 512, 512) for j in range(NT_Q)]
    c_guTh = [VB(B + 28672, 256), VB(B + 32584, 256)]
    c_guTl = [VB(B + 28928, 256), VB(B + 32840, 256)]
    c_gwf = V(B + 29184, 64)
    c_gwh = VB(B + 29248, 32)
    c_gwl = VB(B + 29280, 32)
    c_gateb = V(B + 29312, 8)
    c_out = [V(B + 29328, 1024), V(B + 30352, 1024)]
    c_s = V(B + 31376, 8)
    c_e = V(B + 31384, 8)
    c_G = V(B + 31392, 4)
    c_m8 = V(B + 31400, 8)
    c_rank = V(B + 31408, 8)
    c_sel = V(B + 31416, 8)
    c_sw = V(B + 31424, 8)
    c_scr1 = V(B + 31432, 64)
    c_scr2 = V(B + 31496, 64)
    c_scrA = V(B + 31560, 1024)
    assert B + 33096 <= NAR

    s_identi = V(B + 26624, 128).bitcast(I32)
    s_posi = V(B + 27648, 32).bitcast(I32)
    s_ki = V(B + 25600, 1024).bitcast(I32)
    s_theta = a1_kn
    s_fold = a1_scr
    s_kf = a1_kv[:, 0:1024]
    s_emi4 = a1_kv[:, 1024:1040].bitcast(I32)
    s_emi8 = a1_kv[:, 1040:1104].bitcast(I32)

    PS = [nc.alloc_psum_tensor(f"ps{i}", [128, 512], F32).ap() for i in range(4)]
    PDD = [nc.alloc_psum_tensor(f"pd{i}", [128, 1024], F32).ap() for i in range(2)]

    def vk_bank(h, rows=128):
        g, k = divmod(h, 4)
        return PDD[g // 2][0:rows, (g % 2) * 512 + k * 65:(g % 2) * 512 + k * 65 + 65]

    p = Prog(nc)
    for s in ("sP", "sPG", "sX0", "sX1", "sWA", "sW0", "sW1", "sW2", "sW3",
              "sQ0", "sQ1", "sQ2", "sQ3",
              "sVD", "sY0", "sY1", "sO", "sWS0", "sWS1", "sWB0", "sWB1",
              "sT", "sV", "sS", "sG"):
        p.sem(s)

    def dma(eng, out, in_, sem="sP"):
        h = p.sems[sem]
        p.inc(sem, 16)
        p.on(eng, lambda e, out=out, in_=in_, h=h:
             e.dma_start(out=out, in_=in_).then_inc(h, 16))
        return p.cnt[sem]

    def t_step(ins_fn, inc=True):
        if inc:
            p.on("tensor", p.tag(ins_fn, "sT"))
        else:
            p.on("tensor", ins_fn)
        return p.cnt["sT"]

    def v_step(ins_fn, inc=True):
        p.on("vector", lambda e: e.drain())
        if inc:
            p.on("vector", p.tag(ins_fn, "sV"))
        else:
            p.on("vector", ins_fn)
        return p.cnt["sV"]

    def s_step(ins_fn, inc=True):
        p.on("scalar", lambda e: e.drain())
        if inc:
            p.on("scalar", p.tag(ins_fn, "sS"))
        else:
            p.on("scalar", ins_fn)
        return p.cnt["sS"]

    def g_step(ins_fn, inc=True):
        p.on("gpsimd", lambda e: e.drain())
        if inc:
            p.on("gpsimd", p.tag(ins_fn, "sG"))
        else:
            p.on("gpsimd", ins_fn)
        return p.cnt["sG"]

    def split_v(hi, lo, src, inc=True):
        v_step(lambda e: e.tensor_copy(hi, src), inc=False)
        return v_step(lambda e: e.tensor_tensor(lo, src, hi, op=OP.subtract), inc=inc)

    def split_g(hi, lo, src, inc=True):
        g_step(lambda e: e.tensor_copy(hi, src), inc=False)
        return g_step(lambda e: e.tensor_tensor(lo, src, hi, op=OP.subtract), inc=inc)

    marks = {"ps01": 0}

    def transpose8s(src, dsth, dstl, n=8, src_mark=0):
        assert n % 2 == 0
        p.wait("tensor", "sV", max(marks["ps01"], src_mark))
        for kc in range(n):
            bank = PS[kc % 2]
            off = (kc // 2) * 128
            t_step(lambda e, bank=bank, off=off, kc=kc, src=src:
                   e.transpose(bank[:, off:off + 128],
                               src[:, kc * 128:(kc + 1) * 128], ident),
                   inc=(kc == n - 1))
        p.wait("vector", "sT")
        nb = (n // 2) * 128
        for b in range(2):
            hv = dsth.rearrange("p (a two c) -> p a two c", two=2, c=128)[:, :, b, :]
            lv = dstl.rearrange("p (a two c) -> p a two c", two=2, c=128)[:, :, b, :]
            sv = PS[b][:, 0:nb].rearrange("p (a c) -> p a c", c=128)
            split_v(hv, lv, sv, inc=(b == 1))
        marks["ps01"] = p.cnt["sV"]
        return marks["ps01"]

    def mm3(bank, sh, sl, mh, ml, start, stop, inc=False):
        t_step(lambda e: e.matmul(bank, sh, mh, start=start, stop=False,
                                  skip_group_check=True), inc=False)
        t_step(lambda e: e.matmul(bank, sh, ml, start=False, stop=False,
                                  skip_group_check=True), inc=False)
        return t_step(lambda e: e.matmul(bank, sl, mh, start=False, stop=stop,
                                         skip_group_check=True), inc=inc)

    def w_chunk(kind, e, cc):
        if kind == "g":
            t = wg[e] if e < 8 else swg[:, (e - 8) * 512:(e - 7) * 512]
        elif kind == "u":
            t = wu[e] if e < 8 else swu[:, (e - 8) * 512:(e - 7) * 512]
        else:
            t = wd[e] if e < 8 else swd[(e - 8) * 512:(e - 7) * 512, :]
        if kind in ("g", "u"):
            return t.rearrange("(a p) n -> p a n", p=128)[:, 2 * cc:2 * cc + 2, :]
        return t.rearrange("(a p) n -> p a n", p=128)[:, cc:cc + 1, :]

    slot_key = [(e0, k) for e0 in range(NE) for k in ("g", "u", "d")]
    sp_jobs = [(si, cc) for si in range(NSLOT) for cc in range(4)]
    wsp_wbs = [dict() for _ in range(NSLOT)]

    dx_bkv = dma("sync", a1_bkv[:], bqkv[0:1, DIM:3 * DIM].broadcast_to([128, 2048]))
    dx_gk = dma("sync", a1_gk[:], gkv[0:1, :].broadcast_to([128, DIM]))
    dx_xin = {}
    dx_xin[0] = dma("sync", a1_xin[0][:], xb[0:128, :], sem="sX0")
    dx_poff = dma("sync", poffv, posoff[0:1, :].broadcast_to([128, NT_ALL]))

    g_step(lambda e: e.iota(s_identi, pattern=[[-1, 128]], base=0, channel_multiplier=1))
    p.wait("vector", "sG")
    v_step(lambda e: e.tensor_scalar(ident, s_identi, 0, None, op0=OP.is_equal))
    g_step(lambda e: e.iota(s_emi4, pattern=[[1, 4], [-1, 4]], base=0, channel_multiplier=0))
    g_step(lambda e: e.iota(s_emi8, pattern=[[1, 8], [-1, 8]], base=0, channel_multiplier=0))
    p.wait("vector", "sG")
    v_step(lambda e: e.tensor_scalar(emask4, s_emi4, 0, None, op0=OP.is_gt), inc=False)
    v_step(lambda e: e.tensor_scalar(emask8, s_emi8, 0, None, op0=OP.is_gt), inc=False)
    v_step(lambda e: e.memset(VKTh[:, :], 0.0), inc=False)
    v_step(lambda e: e.memset(VKTl[:, :], 0.0), inc=False)
    v_step(lambda e: e.memset(c_rmseps, RMS_EPS), inc=False)
    v_step(lambda e: e.memset(c_lneps, LN_EPS), inc=False)
    for bi in range(2):
        v_step(lambda e, bi=bi: e.memset(
            a1_vph[bi].rearrange("p (h d) -> p h d", d=65)[:, :, 64:65], 1.0),
            inc=False)
        v_step(lambda e, bi=bi: e.memset(a1_vpl[bi][:, :], 0.0), inc=False)

    g_step(lambda e: e.iota(s_posi, pattern=[[128, NT_ALL]], base=0, channel_multiplier=1))
    p.wait("vector", "sG")
    v_step(lambda e: e.tensor_copy(posf, s_posi), inc=False)
    p.wait("vector", "sP")
    v_step(lambda e: e.tensor_tensor(posf, posf, poffv, op=OP.add))
    for i_ in range(32):
        val = float(np.float32(1.0 / THETA ** (i_ / 32.0)))
        v_step(lambda e, i_=i_, val=val: e.memset(invf[:, i_:i_ + 1], val),
               inc=(i_ == 31))
    v_step(lambda e: e.tensor_tensor(
        s_theta[:].rearrange("p (t i) -> p t i", i=32),
        posf.unsqueeze(2).broadcast_to([128, 32, 32]),
        invf.unsqueeze(1).broadcast_to([128, 32, 32]), op=OP.mult))

    def sincos(dst, shift):
        if shift:
            v_step(lambda e: e.tensor_scalar(s_fold, s_theta, shift, None, op0=OP.add), inc=False)
            src = s_fold
        else:
            src = s_theta
        v_step(lambda e, src=src: e.tensor_scalar(dst, src, 1.0 / TWO_PI, None,
                                                  op0=OP.mult), inc=False)
        v_step(lambda e: e.tensor_copy(s_ki, dst), inc=False)
        v_step(lambda e: e.tensor_copy(s_kf, s_ki), inc=False)
        for ci, c in enumerate((CW1, CW2, CW3)):
            v_step(lambda e, c=c, src=src: e.scalar_tensor_tensor(
                dst, s_kf, -c, src if c == CW1 else dst,
                op0=OP.mult, op1=OP.add), inc=False)
        v_step(lambda e: e.tensor_scalar(s_kf, dst, math.pi, None, op0=OP.is_gt), inc=False)
        v_step(lambda e: e.scalar_tensor_tensor(dst, s_kf, -TWO_PI, dst,
                                                op0=OP.mult, op1=OP.add), inc=False)
        v_step(lambda e: e.tensor_scalar(s_kf, dst, -math.pi, None, op0=OP.is_lt), inc=False)
        v_step(lambda e: e.scalar_tensor_tensor(dst, s_kf, TWO_PI, dst,
                                                op0=OP.mult, op1=OP.add), inc=False)
        v_step(lambda e: e.tensor_scalar(dst, dst, math.pi, None, op0=OP.min), inc=False)
        v_step(lambda e: e.tensor_scalar(dst, dst, -math.pi, None, op0=OP.max))
        p.wait("scalar", "sV")
        s_step(lambda e: e.activation(dst, dst, AF.Sin))

    sincos(sin_tab, 0.0)
    sincos(cos_tab, math.pi / 2.0)
    p.wait("vector", "sS")
    marks["ps01"] = p.cnt["sV"]

    wkv_v = wkvT[:].rearrange("(a p) (b n) -> p a b n", p=128, n=1024)
    dwk = {}
    for c in range(16):
        stg = spw_a if c % 2 == 0 else spw_b
        dwk[c] = dma("gpsimd", stg, wkv_v[:, c // 2, c % 2, :],
                     sem=f"sWS{c % 2}")
        p.wait("gpsimd", f"sWS{c % 2}", dwk[c])
        split_g(a1_WKVh[:, c * 1024:(c + 1) * 1024],
                a1_WKVl[:, c * 1024:(c + 1) * 1024], stg)
    wkv_mark = p.cnt["sG"]

    wq_v = wqT[:].rearrange("(a p) n -> p a n", p=128)
    wo_v = woutT[:].rearrange("(a p) n -> p a n", p=128)
    wq_out_done = 0
    for c in range(16):
        src = wq_v[:, c, :] if c < 8 else wo_v[:, c - 8, :]
        dk = dma("gpsimd", spw_a, src, sem="sWS0")
        p.wait("gpsimd", "sWS0", dk)
        if wq_out_done:
            p.wait("gpsimd", "sWA", wq_out_done)
        gmark = split_g(spw_hi, spw_lo, spw_a)
        p.wait("gpsimd", "sG", gmark)
        dma("gpsimd", wspq[2 * (c // 8), :, (c % 8) * 1024:(c % 8 + 1) * 1024],
            spw_hi, sem="sWA")
        wq_out_done = dma(
            "gpsimd", wspq[2 * (c // 8) + 1, :, (c % 8) * 1024:(c % 8 + 1) * 1024],
            spw_lo, sem="sWA")

    SP_START = 120
    spw_pair = V(B + 31776, 1024).bitcast(BF16)
    a1_wb_last = {}
    for ji in range(SP_START):
        si, cc = sp_jobs[ji]
        e0, kind = slot_key[si]
        nsub = 512 if kind in ("g", "u") else 1024
        dk = dma("gpsimd", spw_a.rearrange("p (a n) -> p a n", n=nsub),
                 w_chunk(kind, e0, cc), sem="sWS0")
        p.wait("gpsimd", "sWS0", dk)
        if ji == 0:
            p.wait("gpsimd", "sWA", wq_out_done)
        else:
            par_p = (ji - 1) % 2
            p.wait("gpsimd", f"sWB{par_p}", a1_wb_last[par_p])
        gmark = split_g(spw_hi, spw_lo, spw_a)
        p.wait("gpsimd", "sG", gmark)
        a1_wb_last[ji % 2] = dma("gpsimd", wsp[si, :, cc * 2048:(cc + 1) * 2048],
                                 spw_pair, sem=f"sWB{ji % 2}")
        wsp_wbs[si][f"sWB{ji % 2}"] = a1_wb_last[ji % 2]

    mark_tr = {}
    mark_vk = {}
    bank_mark = {2: 0, 3: 0}
    kfvp_mark = {0: 0, 1: 0}

    for t in range(NT_ALL):
        if t + 1 < NT_ALL:
            if t - 1 >= 0:
                p.wait("sync", "sT", mark_tr[t - 1])
            dx_xin[t + 1] = dma("sync", a1_xin[(t + 1) % 2][:],
                                xb[(t + 1) * 128:(t + 2) * 128, :],
                                sem=f"sX{(t + 1) % 2}")
        p.wait("tensor", f"sX{t % 2}", dx_xin[t])
        ts_mark = transpose8s(a1_xin[t % 2], a1_xTh, a1_xTl)
        mark_tr[t] = p.cnt["sT"]
        if t >= 1:
            tt = t - 1
            p.wait("tensor", "sV", kfvp_mark[tt % 2])
            for h in range(NH):
                mm3(vk_bank(h, 64),
                    a1_kfh[tt % 2][:, h * 64:(h + 1) * 64],
                    a1_kfl[tt % 2][:, h * 64:(h + 1) * 64],
                    a1_vph[tt % 2][:, h * 65:(h + 1) * 65],
                    a1_vpl[tt % 2][:, h * 65:(h + 1) * 65],
                    start=(tt == 0 and h % 4 == 0), stop=False,
                    inc=(h == NH - 1))
            mark_vk[tt] = p.cnt["sT"]
        if t == 0:
            p.wait("tensor", "sG", wkv_mark)
        for oc in range(4):
            bank = PS[2 + oc % 2]
            p.wait("tensor", "sV", max(bank_mark[2 + oc % 2], ts_mark))
            for kc in range(8):
                mm3(bank,
                    a1_xTh[:, kc * 128:(kc + 1) * 128],
                    a1_xTl[:, kc * 128:(kc + 1) * 128],
                    a1_WKVh[:, kc * 2048 + oc * 512:kc * 2048 + (oc + 1) * 512],
                    a1_WKVl[:, kc * 2048 + oc * 512:kc * 2048 + (oc + 1) * 512],
                    start=(kc == 0), stop=(kc == 7), inc=(kc == 7))
            p.wait("vector", "sT")
            if t == 0 and oc == 0:
                p.wait("vector", "sP")
            bank_mark[2 + oc % 2] = v_step(lambda e, bank=bank, oc=oc: e.tensor_tensor(
                a1_kv[:, oc * 512:(oc + 1) * 512], bank,
                a1_bkv[:, oc * 512:(oc + 1) * 512], op=OP.add))
        p.wait("scalar", "sV")
        s_step(lambda e: e.activation(a1_scr, a1_kv[:, 0:1024], AF.Square,
                                      accum_out=ssq), inc=False)
        s_step(lambda e: e.activation(st1, ssq, AF.Ln, scale=1.0 / DIM,
                                      bias=c_rmseps), inc=False)
        s_step(lambda e: e.activation(rstd, st1, AF.Exp, scale=-0.5))
        p.wait("vector", "sS")
        if t == 0:
            p.wait("vector", "sP")
        v_step(lambda e: e.scalar_tensor_tensor(a1_kn, a1_kv[:, 0:1024], rstd,
                                                a1_gk, op0=OP.mult, op1=OP.mult), inc=False)
        cs = cos_tab[:, t * 32:(t + 1) * 32].unsqueeze(1).broadcast_to([128, 16, 32])
        sn = sin_tab[:, t * 32:(t + 1) * 32].unsqueeze(1).broadcast_to([128, 16, 32])
        knv = a1_kn[:].rearrange("p (h d two) -> p h d two", h=16, two=2)
        kro = a1_kv[:, 0:1024]
        krv = kro.rearrange("p (h d two) -> p h d two", h=16, two=2)
        tr, ti = knv[:, :, :, 0], knv[:, :, :, 1]
        sc1 = a1_scr[:, 0:512].rearrange("p (h d) -> p h d", d=32)
        sc2 = a1_scr[:, 512:1024].rearrange("p (h d) -> p h d", d=32)
        v_step(lambda e, tr=tr, cs=cs, sc1=sc1: e.tensor_tensor(sc1, tr, cs, op=OP.mult), inc=False)
        v_step(lambda e, ti=ti, sn=sn, sc2=sc2: e.tensor_tensor(sc2, ti, sn, op=OP.mult), inc=False)
        v_step(lambda e, krv=krv, sc1=sc1, sc2=sc2: e.tensor_tensor(krv[:, :, :, 0], sc1, sc2, op=OP.subtract), inc=False)
        v_step(lambda e, tr=tr, sn=sn, sc1=sc1: e.tensor_tensor(sc1, tr, sn, op=OP.mult), inc=False)
        v_step(lambda e, ti=ti, cs=cs, sc2=sc2: e.tensor_tensor(sc2, ti, cs, op=OP.mult), inc=False)
        v_step(lambda e, krv=krv, sc1=sc1, sc2=sc2: e.tensor_tensor(krv[:, :, :, 1], sc1, sc2, op=OP.add), inc=False)
        v_step(lambda e, kro=kro: e.tensor_scalar(kro, kro, 0.0, None, op0=OP.max), inc=False)
        split_v(a1_kfh[t % 2][:, :], a1_kfl[t % 2][:, :], kro, inc=False)
        vsrc = a1_kv[:, 1024:2048].rearrange("p (h d) -> p h d", d=64)
        vph = a1_vph[t % 2].rearrange("p (h d) -> p h d", d=65)[:, :, 0:64]
        vpl = a1_vpl[t % 2].rearrange("p (h d) -> p h d", d=65)[:, :, 0:64]
        v_step(lambda e, vph=vph, vsrc=vsrc: e.tensor_copy(vph, vsrc), inc=False)
        kfvp_mark[t % 2] = v_step(lambda e, vpl=vpl, vsrc=vsrc, vph=vph: e.tensor_tensor(vpl, vsrc, vph, op=OP.subtract))

    tt = NT_ALL - 1
    p.wait("tensor", "sV", kfvp_mark[tt % 2])
    for h in range(NH):
        mm3(vk_bank(h, 64),
            a1_kfh[tt % 2][:, h * 64:(h + 1) * 64],
            a1_kfl[tt % 2][:, h * 64:(h + 1) * 64],
            a1_vph[tt % 2][:, h * 65:(h + 1) * 65],
            a1_vpl[tt % 2][:, h * 65:(h + 1) * 65],
            start=False, stop=True, inc=(h == NH - 1))
    mark_vk[tt] = p.cnt["sT"]

    VKS = a1_kv[0:64, 0:1040]
    p.wait("vector", "sT")
    for g in range(4):
        src = PDD[g // 2][:, (g % 2) * 512:(g % 2) * 512 + 260]
        v_step(lambda e, g=g, src=src: e.tensor_copy(
            a1_kv[0:64, g * 260:(g + 1) * 260], src[0:64, :]), inc=False)
    hi_all = a1_kn[0:64, 0:520].bitcast(BF16)
    lo_all = a1_scr[0:64, 0:520].bitcast(BF16)
    split_v(hi_all, lo_all, VKS, inc=False)
    evens = lambda ap: ap.rearrange("p (h two d) -> p h two d", two=2, d=65)[:, :, 0, :]
    odds = lambda ap: ap.rearrange("p (h two d) -> p h two d", two=2, d=65)[:, :, 1, :]
    v_step(lambda e: e.tensor_copy(evens(VKTh[0:64, :]), evens(hi_all)), inc=False)
    v_step(lambda e: e.tensor_copy(evens(VKTl[0:64, :]), evens(lo_all)))
    p.wait("sync", "sV")
    dma("sync", odds(VKTh[64:128, :]), odds(hi_all), sem="sVD")
    dx_vkt = dma("sync", odds(VKTl[64:128, :]), odds(lo_all), sem="sVD")
    if t_vkt is not None:
        p.wait("sync", "sVD")
        dma("sync", t_vkt[:, 0:520], VKTh[:, :].bitcast(F32), sem="sO")
        dma("sync", t_vkt[:, 520:1040], VKTl[:, :].bitcast(F32), sem="sO")
    if t_kf0 is not None:
        dma("sync", t_kf0[:, 0:512], a1_kfh[tt % 2][:, :].bitcast(F32), sem="sO")
        dma("sync", t_kf0[:, 512:1024], a1_kfl[tt % 2][:, :].bitcast(F32), sem="sO")
    if t_vkt is not None or t_kf0 is not None:
        p.wait("vector", "sO")

    if phases == 1:
        for s in ("sY0", "sY1", "sO"):
            p.on("sync", lambda e, s=s: e.wait_ge(p.sems[s], p.cnt[s]) if p.cnt[s] else None)
        p.replay()
        return nc, tap_t

    p.wait("gpsimd", "sT", mark_vk[NT_ALL - 1])
    p.wait("gpsimd", "sV")
    p.wait("gpsimd", "sVD")
    p.wait("gpsimd", "sWA", wq_out_done)
    p.wait("sync", "sT", mark_vk[NT_ALL - 1])
    p.wait("sync", "sV")
    dx_bq = dma("sync", a2_bq[:], bqkv[0:1, 0:DIM].broadcast_to([128, DIM]))
    dx_gq = dma("sync", a2_gq[:], gqv[0:1, :].broadcast_to([128, DIM]))
    dx_bout = dma("sync", a2_bout[:], boutv[0:1, :].broadcast_to([128, DIM]))
    dx_gout = dma("sync", a2_gout[:], goutv[0:1, :].broadcast_to([128, DIM]))

    p.wait("sync", "sWA", wq_out_done)
    dq_wq = [dma("sync", a2_WQh[:, :], wspq[0], sem="sQ0"),
             dma("sync", a2_WQl[:, :], wspq[1], sem="sQ1"),
             dma("sync", a2_WOh[:, :], wspq[2], sem="sQ2"),
             dma("sync", a2_WOl[:, :], wspq[3], sem="sQ3")]

    sp_state = {"i": SP_START}
    sp_wb = dict(a1_wb_last)
    sp_dk = {}

    def sp_in_dma(ji):
        si, cc = sp_jobs[ji]
        e0, kind = slot_key[si]
        nsub = 512 if kind in ("g", "u") else 1024
        sp_dk[ji] = dma("gpsimd",
                        sp_in[ji % 2].rearrange("p (a n) -> p a n", n=nsub),
                        w_chunk(kind, e0, cc), sem=f"sWS{ji % 2}")

    def emit_split_chunks(nmax):
        for _ in range(nmax):
            if sp_state["i"] >= len(sp_jobs):
                return
            ji = sp_state["i"]
            sp_state["i"] += 1
            si, cc = sp_jobs[ji]
            par = ji % 2
            if ji not in sp_dk:
                sp_in_dma(ji)
            if ji + 1 < len(sp_jobs):
                sp_in_dma(ji + 1)
            p.wait("gpsimd", f"sWS{par}", sp_dk[ji])
            if sp_wb.get(par, 0):
                p.wait("gpsimd", f"sWB{par}", sp_wb[par])
            gmark = split_g(sp_out[par][:, 0:512].bitcast(BF16),
                            sp_out[par][:, 512:1024].bitcast(BF16), sp_in[par])
            p.wait("gpsimd", "sG", gmark)
            sp_wb[par] = dma("gpsimd", wsp[si, :, cc * 2048:(cc + 1) * 2048],
                             sp_out[par][:, :].bitcast(BF16), sem=f"sWB{par}")
            wsp_wbs[si][f"sWB{par}"] = sp_wb[par]

    dx_xin2 = {}
    dx_xin2[0] = dma("sync", a2_xin[0][:], xb[0:128, :], sem="sX0")
    mark_tr2 = {}
    mark_res2 = {}
    mark_x2 = {}
    div_mark = 0

    for i in range(NT_OWN):
        if i + 1 < NT_OWN:
            if i - 1 >= 0:
                p.wait("sync", "sV", mark_res2[i - 1])
            dx_xin2[i + 1] = dma("sync", a2_xin[(i + 1) % 2][:],
                                 xb[(i + 1) * 128:(i + 2) * 128, :],
                                 sem=f"sX{(i + 1) % 2}")
        p.wait("tensor", f"sX{i % 2}", dx_xin2[i])
        if i == 0:
            p.wait("tensor", "sVD", dx_vkt)
            for si_, dv_ in enumerate(dq_wq):
                p.wait("tensor", f"sQ{si_}", dv_)
        ts_mark = transpose8s(a2_xin[i % 2], a2_TSh, a2_TSl)
        mark_tr2[i] = p.cnt["sT"]
        for oc in range(2):
            bank = PS[2 + oc]
            p.wait("tensor", "sV", max(bank_mark[2 + oc], ts_mark))
            for kc in range(8):
                mm3(bank,
                    a2_TSh[:, kc * 128:(kc + 1) * 128],
                    a2_TSl[:, kc * 128:(kc + 1) * 128],
                    a2_WQh[:, kc * 1024 + oc * 512:kc * 1024 + (oc + 1) * 512],
                    a2_WQl[:, kc * 1024 + oc * 512:kc * 1024 + (oc + 1) * 512],
                    start=(kc == 0), stop=(kc == 7), inc=(kc == 7))
        p.wait("vector", "sT")
        if i == 0:
            p.wait("vector", "sP")
        for oc in range(2):
            bank_mark[2 + oc] = v_step(lambda e, oc=oc: e.tensor_tensor(
                a2_b1[:, oc * 512:(oc + 1) * 512], PS[2 + oc],
                a2_bq[:, oc * 512:(oc + 1) * 512], op=OP.add), inc=True)
        p.wait("scalar", "sV")
        s_step(lambda e: e.activation(a2_scr, a2_b1, AF.Square, accum_out=ssq), inc=False)
        s_step(lambda e: e.activation(st1, ssq, AF.Ln, scale=1.0 / DIM, bias=c_rmseps), inc=False)
        s_step(lambda e: e.activation(rstd, st1, AF.Exp, scale=-0.5))
        p.wait("vector", "sS")
        v_step(lambda e: e.scalar_tensor_tensor(a2_b2, a2_b1, rstd, a2_gq,
                                                op0=OP.mult, op1=OP.mult), inc=False)
        cs = cos_tab[:, i * 32:(i + 1) * 32].unsqueeze(1).broadcast_to([128, 16, 32])
        sn = sin_tab[:, i * 32:(i + 1) * 32].unsqueeze(1).broadcast_to([128, 16, 32])
        qnv = a2_b2[:].rearrange("p (h d two) -> p h d two", h=16, two=2)
        qfv = a2_b3[:].rearrange("p (h d two) -> p h d two", h=16, two=2)
        tr, ti = qnv[:, :, :, 0], qnv[:, :, :, 1]
        sc1 = a2_scr[:, 0:512].rearrange("p (h d) -> p h d", d=32)
        sc2 = a2_scr[:, 512:1024].rearrange("p (h d) -> p h d", d=32)
        v_step(lambda e, tr=tr, cs=cs, sc1=sc1: e.tensor_tensor(sc1, tr, cs, op=OP.mult), inc=False)
        v_step(lambda e, ti=ti, sn=sn, sc2=sc2: e.tensor_tensor(sc2, ti, sn, op=OP.mult), inc=False)
        v_step(lambda e, qfv=qfv, sc1=sc1, sc2=sc2: e.tensor_tensor(qfv[:, :, :, 0], sc1, sc2, op=OP.subtract), inc=False)
        v_step(lambda e, tr=tr, sn=sn, sc1=sc1: e.tensor_tensor(sc1, tr, sn, op=OP.mult), inc=False)
        v_step(lambda e, ti=ti, cs=cs, sc2=sc2: e.tensor_tensor(sc2, ti, cs, op=OP.mult), inc=False)
        v_step(lambda e, qfv=qfv, sc1=sc1, sc2=sc2: e.tensor_tensor(qfv[:, :, :, 1], sc1, sc2, op=OP.add), inc=False)
        qf_mark = v_step(lambda e: e.tensor_scalar(a2_b3[:], a2_b3[:], 0.0, None, op0=OP.max))
        if i == 0 and t_qf0 is not None:
            p.wait("sync", "sV")
            dma("sync", t_qf0[:], a2_b3[:], sem="sO")
            p.wait("vector", "sO")
        ts_mark = transpose8s(a2_b3, a2_TSh, a2_TSl, src_mark=qf_mark)
        p.wait("tensor", "sV", max(div_mark, ts_mark))
        for h in range(NH):
            mm3(vk_bank(h),
                a2_TSh[:, 128 * (h // 2):128 * (h // 2) + 128],
                a2_TSl[:, 128 * (h // 2):128 * (h // 2) + 128],
                VKTh[:, 65 * h:65 * h + 65],
                VKTl[:, 65 * h:65 * h + 65],
                start=True, stop=True, inc=(h == NH - 1))
        p.wait("vector", "sT")
        for g in range(4):
            src = PDD[g // 2][:, (g % 2) * 512:(g % 2) * 512 + 512]
            v_step(lambda e, g=g, src=src: e.tensor_scalar(
                dnm[:, g * 4:(g + 1) * 4],
                src[:, 0:260].rearrange("p (h d) -> p h d", d=65)[:, :, 64],
                EPS_LIN, None, op0=OP.add), inc=False)
        v_step(lambda e: e.reciprocal(rcp, dnm), inc=False)
        for g in range(4):
            src = PDD[g // 2][:, (g % 2) * 512:(g % 2) * 512 + 512]
            v_step(lambda e, g=g, src=src: e.tensor_tensor(
                a2_b4[:, g * 256:(g + 1) * 256].rearrange("p (h d) -> p h d", d=64),
                src[:, 0:260].rearrange("p (h d) -> p h d", d=65)[:, :, 0:64],
                rcp[:, g * 4:(g + 1) * 4].unsqueeze(2).broadcast_to([128, 4, 64]),
                op=OP.mult), inc=(g == 3))
        div_mark = p.cnt["sV"]
        if i == 0 and t_attn0 is not None:
            p.wait("sync", "sV")
            dma("sync", t_attn0[:], a2_b4[:], sem="sO")
            p.wait("vector", "sO")
        ts_mark = transpose8s(a2_b4, a2_TSh, a2_TSl, src_mark=div_mark)
        for oc in range(2):
            bank = PS[2 + oc]
            p.wait("tensor", "sV", max(bank_mark[2 + oc], ts_mark))
            for kc in range(8):
                mm3(bank,
                    a2_TSh[:, kc * 128:(kc + 1) * 128],
                    a2_TSl[:, kc * 128:(kc + 1) * 128],
                    a2_WOh[:, kc * 1024 + oc * 512:kc * 1024 + (oc + 1) * 512],
                    a2_WOl[:, kc * 1024 + oc * 512:kc * 1024 + (oc + 1) * 512],
                    start=(kc == 0), stop=(kc == 7), inc=(kc == 7))
        p.wait("vector", "sT")
        for oc in range(2):
            bank_mark[2 + oc] = v_step(lambda e, oc=oc: e.tensor_tensor(
                a2_b1[:, oc * 512:(oc + 1) * 512], PS[2 + oc],
                a2_bout[:, oc * 512:(oc + 1) * 512], op=OP.add), inc=True)
        p.wait("scalar", "sV")
        s_step(lambda e: e.activation(a2_scr, a2_b1, AF.Square, accum_out=ssq), inc=False)
        s_step(lambda e: e.activation(st1, ssq, AF.Ln, scale=1.0 / DIM, bias=c_rmseps), inc=False)
        s_step(lambda e: e.activation(rstd, st1, AF.Exp, scale=-0.5))
        p.wait("vector", "sS")
        v_step(lambda e: e.scalar_tensor_tensor(a2_b2, a2_b1, rstd, a2_gout,
                                                op0=OP.mult, op1=OP.mult), inc=False)
        v_step(lambda e, i=i: e.tensor_tensor(a2_b3[:], a2_b2, a2_xin[i % 2][:], op=OP.add))
        mark_res2[i] = p.cnt["sV"]
        p.wait("scalar", "sV")
        s_step(lambda e: e.activation(a2_scr, a2_b3[:], AF.Copy, accum_out=ssum), inc=False)
        s_step(lambda e: e.activation(a2_scr, a2_b3[:], AF.Square, accum_out=ssq))
        p.wait("vector", "sS")
        v_step(lambda e: e.tensor_scalar(smean, ssum, 1.0 / DIM, None, op0=OP.mult), inc=False)
        v_step(lambda e: e.tensor_tensor(st1, smean, smean, op=OP.mult), inc=False)
        v_step(lambda e: e.scalar_tensor_tensor(svar, ssq, 1.0 / DIM, st1,
                                                op0=OP.mult, op1=OP.subtract))
        p.wait("scalar", "sV")
        s_step(lambda e: e.activation(st1, svar, AF.Ln, bias=c_lneps), inc=False)
        s_step(lambda e: e.activation(rstd, st1, AF.Exp, scale=-0.5))
        p.wait("vector", "sS")
        mark_x2[i] = v_step(lambda e, i=i: e.tensor_scalar(X2[i], a2_b3[:], smean, rstd,
                                              op0=OP.subtract, op1=OP.mult))
        if phases == 3:
            emit_split_chunks(8)

    if phases == 3:
        emit_split_chunks(len(sp_jobs))

    if t_x2 is not None:
        p.wait("sync", "sV")
        for i in range(NT_OWN):
            dma("sync", t_x2[i * 128:(i + 1) * 128, :], X2[i], sem="sO")
        p.wait("vector", "sO")

    if phases == 2:
        for s in ("sY0", "sY1", "sO"):
            p.on("sync", lambda e, s=s: e.wait_ge(p.sems[s], p.cnt[s]) if p.cnt[s] else None)
        p.replay()
        return nc, tap_t

    p.wait("gpsimd", "sT")
    p.wait("gpsimd", "sV")
    dx_gw = dma("gpsimd", c_gwf[:].rearrange("p (c n) -> p c n", n=8),
                gatew[:].rearrange("(c p) n -> p c n", p=128), sem="sPG")
    dx_gateb = dma("gpsimd", c_gateb[:], gateb[0:1, :].broadcast_to([128, 8]), sem="sPG")
    p.wait("vector", "sPG")
    p.wait("vector", "sWB0")
    p.wait("vector", "sWB1")
    p.wait("scalar", "sWB0")
    p.wait("scalar", "sWB1")
    gw_split_v = split_v(c_gwh[:, :], c_gwl[:, :], c_gwf[:])

    slot_mark = [0, 0, 0, 0]
    load_val = {}
    load_q = [(qq, si) for qq in range(NQ) for si in range(NSLOT)]
    slot_of = {}
    qstate = {"qi": 0}
    ring_started = {"done": False}

    def emit_next_load(s):
        if qstate["qi"] >= len(load_q):
            return
        key = load_q[qstate["qi"]]
        qstate["qi"] += 1
        p.wait("gpsimd", "sT", slot_mark[s])
        _, si = key
        for semn, val in wsp_wbs[si].items():
            p.wait("gpsimd", semn, val)
        load_val[key] = dma("gpsimd", c_ring[s][:, :], wsp[si], sem=f"sW{s}")
        slot_of[key] = s

    for s in range(4):
        emit_next_load(s)

    def free_slot(qq, si):
        s = slot_of[(qq, si)]
        slot_mark[s] = p.cnt["sT"]
        emit_next_load(s)

    def ring_views(slot, kind):
        sb = c_ring[slot]
        if kind in ("g", "u"):
            def wh(c):
                cc, off = divmod(c, 2)
                return sb[:, cc * 2048 + off * 512:cc * 2048 + off * 512 + 512]
            def wl(c):
                cc, off = divmod(c, 2)
                return sb[:, cc * 2048 + 1024 + off * 512:cc * 2048 + 1024 + off * 512 + 512]
        else:
            def wh(c, oc):
                return sb[:, c * 2048 + oc * 512:c * 2048 + oc * 512 + 512]
            def wl(c, oc):
                return sb[:, c * 2048 + 1024 + oc * 512:c * 2048 + 1024 + oc * 512 + 512]
        return wh, wl

    out_dma = {}
    s_markPS = {2: 0, 3: 0}
    v_markPS = {2: 0, 3: 0}
    yq_mark = {0: 0, 1: 0}
    gu_mark = {0: 0, 1: 0}
    def prep_quarter(q):
        tiles = [q * NT_Q + j for j in range(NT_Q)]
        for j, ti_ in enumerate(tiles):
            xtq_mark = transpose8s(X2[ti_], c_xTqh[:, j * 1024:(j + 1) * 1024],
                                   c_xTql[:, j * 1024:(j + 1) * 1024],
                                   src_mark=mark_x2.get(ti_, 0))
        p.wait("tensor", "sPG")
        p.wait("tensor", "sV", xtq_mark)
        for j in range(NT_Q):
            gb = 2 + j % 2
            p.wait("tensor", "sS", s_markPS[gb])
            p.wait("tensor", "sV", v_markPS[gb])
            for kc in range(8):
                mm3(PS[gb][:, 0:8],
                    c_xTqh[:, j * 1024 + kc * 128:j * 1024 + (kc + 1) * 128],
                    c_xTql[:, j * 1024 + kc * 128:j * 1024 + (kc + 1) * 128],
                    c_gwh[:, kc * 8:(kc + 1) * 8],
                    c_gwl[:, kc * 8:(kc + 1) * 8],
                    start=(kc == 0), stop=(kc == 7), inc=(kc == 7))
            p.wait("scalar", "sT")
            p.wait("scalar", "sV")
            s_markPS[gb] = s_step(lambda e, gb=gb: e.activation(c_e, PS[gb][:, 0:8], AF.Exp, scale=-1.0))
            p.wait("vector", "sS")
            v_step(lambda e: e.tensor_scalar(c_e, c_e, 1.0, None, op0=OP.add), inc=False)
            v_step(lambda e: e.reciprocal(c_s, c_e), inc=False)
            v_step(lambda e: e.tensor_tensor(c_s, c_s, c_gateb, op=OP.add), inc=False)
            sv = c_s.rearrange("p (g t) -> p g t", t=2)
            v_step(lambda e, sv=sv: e.tensor_tensor(c_G, sv[:, :, 0], sv[:, :, 1], op=OP.add), inc=False)
            gj = c_G.unsqueeze(1).broadcast_to([128, 4, 4])
            gi = c_G.unsqueeze(2).broadcast_to([128, 4, 4])
            v_step(lambda e, gj=gj, gi=gi: e.tensor_tensor(c_scr1[:, 0:16], gj, gi, op=OP.is_gt), inc=False)
            v_step(lambda e, gj=gj, gi=gi: e.tensor_tensor(c_scr2[:, 0:16], gj, gi, op=OP.is_equal), inc=False)
            v_step(lambda e: e.tensor_tensor(c_scr2[:, 0:16], c_scr2[:, 0:16], emask4, op=OP.mult), inc=False)
            v_step(lambda e: e.tensor_tensor(c_scr1[:, 0:16], c_scr1[:, 0:16], c_scr2[:, 0:16], op=OP.add), inc=False)
            v_step(lambda e: e.tensor_reduce(c_rank[:, 0:4], c_scr1[:, 0:16].rearrange("p (i j) -> p i j", j=4), axis=AX.X, op=OP.add), inc=False)
            v_step(lambda e: e.tensor_scalar(c_rank[:, 0:4], c_rank[:, 0:4], 1.5, None, op0=OP.is_lt), inc=False)
            gm = c_rank[:, 0:4].unsqueeze(2).broadcast_to([128, 4, 2])
            v_step(lambda e, gm=gm: e.tensor_tensor(c_m8.rearrange("p (g t) -> p g t", t=2), c_s.rearrange("p (g t) -> p g t", t=2), gm, op=OP.mult), inc=False)
            mj = c_m8.unsqueeze(1).broadcast_to([128, 8, 8])
            mi = c_m8.unsqueeze(2).broadcast_to([128, 8, 8])
            v_step(lambda e, mj=mj, mi=mi: e.tensor_tensor(c_scr1, mj, mi, op=OP.is_gt), inc=False)
            v_step(lambda e, mj=mj, mi=mi: e.tensor_tensor(c_scr2, mj, mi, op=OP.is_equal), inc=False)
            v_step(lambda e: e.tensor_tensor(c_scr2, c_scr2, emask8, op=OP.mult), inc=False)
            v_step(lambda e: e.tensor_tensor(c_scr1, c_scr1, c_scr2, op=OP.add), inc=False)
            v_step(lambda e: e.tensor_reduce(c_rank, c_scr1.rearrange("p (i j) -> p i j", j=8), axis=AX.X, op=OP.add), inc=False)
            v_step(lambda e: e.tensor_scalar(c_sel, c_rank, 1.5, None, op0=OP.is_lt), inc=False)
            v_step(lambda e: e.tensor_tensor(c_sw, c_sel, c_s, op=OP.mult), inc=False)
            v_step(lambda e: e.tensor_reduce(wsum, c_sw, axis=AX.X, op=OP.add), inc=False)
            v_step(lambda e: e.tensor_scalar(wsum, wsum, 1e-20, None, op0=OP.add), inc=False)
            v_step(lambda e: e.reciprocal(wnorm, wsum), inc=False)
            v_step(lambda e: e.tensor_scalar(wnorm, wnorm, ROUTED_SCALE, None, op0=OP.mult), inc=False)
            v_step(lambda e, j=j: e.tensor_scalar(WD[:, j * 8:(j + 1) * 8], c_sw, wnorm, None, op0=OP.mult))
        if t_wd is not None:
            p.wait("sync", "sV")
            dma("sync", t_wd[:, q * 32:(q + 1) * 32], WD[:], sem="sO")
            p.wait("vector", "sO")


    prep_quarter(0)
    for q in range(NQ):
        tiles = [q * NT_Q + j for j in range(NT_Q)]
        for e_i in range(NE):
            routed = e_i < 8
            si_g = e_i * 3
            si_u = e_i * 3 + 1
            si_d = e_i * 3 + 2
            gwh_, gwl_ = ring_views(slot_of[(q, si_g)], "g")
            uwh_, uwl_ = ring_views(slot_of[(q, si_u)], "u")
            dwh_, dwl_ = ring_views(slot_of[(q, si_d)], "d")
            p.wait("tensor", f"sW{slot_of[(q, si_g)]}", load_val[(q, si_g)])
            for j in range(NT_Q):
                gb = 2 + j % 2
                p.wait("tensor", "sS", s_markPS[gb])
                p.wait("tensor", "sV", v_markPS[gb])
                for kc in range(8):
                    mm3(PS[gb],
                        c_xTqh[:, j * 1024 + kc * 128:j * 1024 + (kc + 1) * 128],
                        c_xTql[:, j * 1024 + kc * 128:j * 1024 + (kc + 1) * 128],
                        gwh_(kc), gwl_(kc),
                        start=(kc == 0), stop=(kc == 7), inc=(kc == 7))
                p.wait("scalar", "sT")
                p.wait("scalar", "sV", gu_mark[j % 2])
                s_markPS[gb] = s_step(lambda e, j=j, gb=gb: e.activation(c_g[j], PS[gb], AF.Silu))
            free_slot(q, si_g)
            p.wait("tensor", f"sW{slot_of[(q, si_u)]}", load_val[(q, si_u)])
            for j in range(NT_Q):
                gb = 2 + j % 2
                p.wait("tensor", "sV", v_markPS[gb])
                p.wait("tensor", "sS", s_markPS[gb])
                for kc in range(8):
                    mm3(PS[gb],
                        c_xTqh[:, j * 1024 + kc * 128:j * 1024 + (kc + 1) * 128],
                        c_xTql[:, j * 1024 + kc * 128:j * 1024 + (kc + 1) * 128],
                        uwh_(kc), uwl_(kc),
                        start=(kc == 0), stop=(kc == 7), inc=(kc == 7))
                p.wait("vector", "sT")
                p.wait("vector", "sS")
                if routed:
                    m_ = v_step(lambda e, j=j, e_i=e_i, gb=gb: e.scalar_tensor_tensor(
                        c_gu[j], PS[gb], WD[:, j * 8 + e_i:j * 8 + e_i + 1],
                        c_g[j], op0=OP.mult, op1=OP.mult))
                else:
                    m_ = v_step(lambda e, j=j, gb=gb: e.tensor_tensor(c_gu[j], PS[gb], c_g[j], op=OP.mult))
                v_markPS[gb] = m_
                gu_mark[j % 2] = m_
            free_slot(q, si_u)
            p.wait("tensor", f"sW{slot_of[(q, si_d)]}", load_val[(q, si_d)])
            for j in range(NT_Q):
                db = j % 2
                p.wait("tensor", "sV", max(marks["ps01"], gu_mark[j % 2]))
                for kc in range(4):
                    t_step(lambda e, j=j, kc=kc, db=db: e.transpose(
                        PS[db][:, kc * 128:(kc + 1) * 128],
                        c_gu[j][:, kc * 128:(kc + 1) * 128], ident),
                        inc=(kc == 3))
                p.wait("vector", "sT")
                guts = split_v(c_guTh[db][:, :], c_guTl[db][:, :], PS[db])
                marks["ps01"] = guts
                p.wait("tensor", "sV", max(guts, yq_mark[db]))
                for oc in range(2):
                    for kc in range(4):
                        mm3(PDD[db][:, oc * 512:(oc + 1) * 512],
                            c_guTh[db][:, kc * 128:(kc + 1) * 128],
                            c_guTl[db][:, kc * 128:(kc + 1) * 128],
                            dwh_(kc, oc), dwl_(kc, oc),
                            start=(kc == 0), stop=(kc == 3),
                            inc=(oc == 1 and kc == 3))
                p.wait("vector", "sT")
                if e_i == 0:
                    yq_mark[db] = v_step(lambda e, j=j, db=db: e.tensor_copy(c_Yq[j], PDD[db][:, 0:1024]))
                else:
                    yq_mark[db] = v_step(lambda e, j=j, db=db: e.tensor_tensor(c_Yq[j], c_Yq[j], PDD[db][:, 0:1024], op=OP.add))
            free_slot(q, si_d)


        if q == 0 and t_y0 is not None:
            p.wait("sync", "sV")
            dma("sync", t_y0[:], c_Yq[0], sem="sO")
            p.wait("vector", "sO")

        if q + 1 < NQ:
            prep_quarter(q + 1)

        for j, ti_ in enumerate(tiles):
            ob = c_out[j % 2]
            if q * NT_Q + j >= 2:
                p.wait("vector", f"sY{j % 2}", out_dma.get(q * NT_Q + j - 2, 0))
            v_step(lambda e, j=j, ti_=ti_, ob=ob: e.tensor_tensor(ob[:], c_Yq[j], X2[ti_], op=OP.add))
            p.wait("scalar", "sV")
            s_step(lambda e, ob=ob: e.activation(c_scrA, ob[:], AF.Copy, accum_out=ssum), inc=False)
            s_step(lambda e, ob=ob: e.activation(c_scrA, ob[:], AF.Square, accum_out=ssq))
            p.wait("vector", "sS")
            v_step(lambda e: e.tensor_scalar(smean, ssum, 1.0 / DIM, None, op0=OP.mult), inc=False)
            v_step(lambda e: e.tensor_tensor(st1, smean, smean, op=OP.mult), inc=False)
            v_step(lambda e: e.scalar_tensor_tensor(svar, ssq, 1.0 / DIM, st1,
                                                    op0=OP.mult, op1=OP.subtract))
            p.wait("scalar", "sV")
            s_step(lambda e: e.activation(st1, svar, AF.Ln, bias=c_lneps), inc=False)
            s_step(lambda e: e.activation(rstd, st1, AF.Exp, scale=-0.5))
            p.wait("vector", "sS")
            v_step(lambda e, ob=ob: e.tensor_scalar(ob[:], ob[:], smean, rstd,
                                                    op0=OP.subtract, op1=OP.mult))
            p.wait("sync", "sV")
            out_dma[q * NT_Q + j] = dma("sync", y[ti_ * 128:(ti_ + 1) * 128, :],
                                        ob[:], sem=f"sY{j % 2}")

    for s in ("sY0", "sY1", "sO"):
        p.on("sync", lambda e, s=s: e.wait_ge(p.sems[s], p.cnt[s]) if p.cnt[s] else None)

    p.replay()
    return nc, tap_t


def kernel(**inputs):
    x = np.asarray(inputs["L0_attn_x"])
    Wqkv = np.asarray(inputs["L0_attn_Wqkv"])
    bqkv = np.asarray(inputs["L0_attn_bqkv"])
    gq = np.asarray(inputs["L0_attn_gq"])
    gk = np.asarray(inputs["L0_attn_gk"])
    Wout = np.asarray(inputs["L0_attn_Wout"])
    bout = np.asarray(inputs["L0_attn_bout"])
    gout = np.asarray(inputs["L0_attn_gout"])
    gate_w = np.asarray(inputs["L0_ffn_gate_w"])
    gate_b = np.asarray(inputs["L0_ffn_gate_bias"])
    w_gate = np.asarray(inputs["L0_ffn_w_gate"])
    w_up = np.asarray(inputs["L0_ffn_w_up"])
    w_down = np.asarray(inputs["L0_ffn_w_down"])
    sw_gate = np.asarray(inputs["L0_ffn_sw_gate"])
    sw_up = np.asarray(inputs["L0_ffn_sw_up"])
    sw_down = np.asarray(inputs["L0_ffn_sw_down"])

    nc, _ = build_program()
    C = np.ascontiguousarray
    shared = {
        "wkvT": C(Wqkv[DIM:3 * DIM].T),
        "wqT": C(Wqkv[0:DIM].T),
        "woutT": C(Wout.T),
        "bqkv": C(bqkv.reshape(1, 3 * DIM)),
        "gqv": C(gq.reshape(1, DIM)),
        "gkv": C(gk.reshape(1, DIM)),
        "goutv": C(gout.reshape(1, DIM)),
        "boutv": C(bout.reshape(1, DIM)),
        "gatew": C(gate_w),
        "gateb": C(gate_b.reshape(1, 8)),
        "wg": C(w_gate),
        "wu": C(w_up),
        "wd": C(w_down),
        "swg": C(sw_gate),
        "swu": C(sw_up),
        "swd": C(sw_down),
    }
    in_maps = []
    for c in range(8):
        b, h = divmod(c, 2)
        own = x[b, h * 2048:(h + 1) * 2048]
        fore = x[b, (1 - h) * 2048:(2 - h) * 2048]
        xbc = np.concatenate([own, fore], axis=0)
        poff = np.zeros((1, NT_ALL), dtype=np.float32)
        poff[0, :16] = h * 2048
        poff[0, 16:] = (1 - h) * 2048 - 2048
        m = dict(shared)
        m["xb"] = C(xbc)
        m["posoff"] = poff
        in_maps.append(m)

    res = run_bass_kernel_spmd(nc, in_maps, core_ids=list(range(8)))
    out = np.empty((4, 4096, DIM), dtype=np.float32)
    for c in range(8):
        b, h = divmod(c, 2)
        out[b, h * 2048:(h + 1) * 2048] = res.results[c]["y"]
    return out


# version 17
# speedup vs baseline: 1.0123x; 1.0123x over previous
import math
import numpy as np
import concourse.bass as bass
import concourse.mybir as mybir
from concourse.alu_op_type import AluOpType as OP
from concourse.bass_utils import run_bass_kernel_spmd

F32 = mybir.dt.float32
I32 = mybir.dt.int32
BF16 = mybir.dt.bfloat16
AF = mybir.ActivationFunctionType
AX = mybir.AxisListType

DIM = 1024
NH = 16
RMS_EPS = 1.1920929e-07
EPS_LIN = 1e-06
LN_EPS = 1e-05
THETA = 10000.0
ROUTED_SCALE = 2.5
NT_ALL = 32
NT_OWN = 16
NQ = 4
NT_Q = NT_OWN // NQ
NE = 10
NSLOT = 3 * NE

TWO_PI = 2.0 * math.pi
CW1 = 6.28125
CW2 = float(np.float32(TWO_PI - CW1))
CW3 = float(TWO_PI - CW1 - CW2)

ENGINES = ("sync", "gpsimd", "vector", "scalar", "tensor")


class Prog:

    def __init__(self, nc):
        self.nc = nc
        self.steps = {e: [] for e in ENGINES}
        self.sems = {}
        self.cnt = {}

    def sem(self, name):
        h = self.nc.alloc_semaphore(name)
        self.sems[name] = h
        self.cnt[name] = 0
        return h

    def on(self, eng, fn):
        self.steps[eng].append(fn)

    def wait(self, eng, sem_name, val=None):
        if val is None:
            val = self.cnt[sem_name]
        h = self.sems[sem_name]
        if val > 0:
            self.on(eng, lambda e, h=h, v=val: e.wait_ge(h, v))

    def inc(self, sem_name, amount=1):
        self.cnt[sem_name] += amount
        return self.cnt[sem_name]

    def tag(self, ins_fn, sem_name, amount=1):
        import traceback
        h = self.sems[sem_name]
        self.inc(sem_name, amount)
        site = "".join(traceback.format_stack(limit=6)[:-1])

        def f(e, ins_fn=ins_fn, h=h, amount=amount, site=site):
            ins = ins_fn(e)
            if ins is None:
                raise RuntimeError("op returned None; emitted at:\n" + site)
            ins.then_inc(h, amount)
        return f

    def replay(self):
        nc = self.nc
        with nc.Block() as block:
            @block.sync
            def _(e):
                for f in self.steps["sync"]:
                    f(e)

            @block.gpsimd
            def _(e):
                for f in self.steps["gpsimd"]:
                    f(e)

            @block.vector
            def _(e):
                for f in self.steps["vector"]:
                    f(e)

            @block.scalar
            def _(e):
                for f in self.steps["scalar"]:
                    f(e)

            @block.tensor
            def _(e):
                for f in self.steps["tensor"]:
                    f(e)


def build_program(taps=(), phases=3, race_safe=False):
    taps = set(taps)
    nc = bass.Bass("TRN2", target_bir_lowering=False)

    xb = nc.dram_tensor("xb", [NT_ALL * 128, DIM], F32, kind="ExternalInput").ap()
    wkvT = nc.dram_tensor("wkvT", [DIM, 2 * DIM], F32, kind="ExternalInput").ap()
    wqT = nc.dram_tensor("wqT", [DIM, DIM], F32, kind="ExternalInput").ap()
    woutT = nc.dram_tensor("woutT", [DIM, DIM], F32, kind="ExternalInput").ap()
    bqkv = nc.dram_tensor("bqkv", [1, 3 * DIM], F32, kind="ExternalInput").ap()
    gqv = nc.dram_tensor("gqv", [1, DIM], F32, kind="ExternalInput").ap()
    gkv = nc.dram_tensor("gkv", [1, DIM], F32, kind="ExternalInput").ap()
    goutv = nc.dram_tensor("goutv", [1, DIM], F32, kind="ExternalInput").ap()
    boutv = nc.dram_tensor("boutv", [1, DIM], F32, kind="ExternalInput").ap()
    gatew = nc.dram_tensor("gatew", [DIM, 8], F32, kind="ExternalInput").ap()
    gateb = nc.dram_tensor("gateb", [1, 8], F32, kind="ExternalInput").ap()
    wg = nc.dram_tensor("wg", [8, DIM, 512], F32, kind="ExternalInput").ap()
    wu = nc.dram_tensor("wu", [8, DIM, 512], F32, kind="ExternalInput").ap()
    wd = nc.dram_tensor("wd", [8, 512, DIM], F32, kind="ExternalInput").ap()
    swg = nc.dram_tensor("swg", [DIM, DIM], F32, kind="ExternalInput").ap()
    swu = nc.dram_tensor("swu", [DIM, DIM], F32, kind="ExternalInput").ap()
    swd = nc.dram_tensor("swd", [DIM, DIM], F32, kind="ExternalInput").ap()
    posoff = nc.dram_tensor("posoff", [1, NT_ALL], F32, kind="ExternalInput").ap()
    y = nc.dram_tensor("y", [NT_OWN * 128, DIM], F32, kind="ExternalOutput").ap()
    wsp = nc.dram_tensor("wsp", [NSLOT, 128, 8192], BF16, kind="Internal").ap()
    wspq = nc.dram_tensor("wspq", [4, 128, 8192], BF16, kind="Internal").ap()

    tap_t = {}
    def tap(name, shape):
        if name in taps:
            tap_t[name] = nc.dram_tensor("tap_" + name, shape, F32,
                                         kind="ExternalOutput").ap()
            return tap_t[name]
        return None

    t_vkt = tap("vkt", [128, NH * 65])
    t_kf0 = tap("kf0", [128, DIM])
    t_x2 = tap("x2", [NT_OWN * 128, DIM])
    t_wd = tap("wdense", [128, NQ * NT_Q * 8])
    t_qf0 = tap("qf0", [128, DIM])
    t_attn0 = tap("attn0", [128, DIM])
    t_y0 = tap("ffny0", [128, DIM])

    NAR = 53200
    AR = nc.alloc_sbuf_tensor("AR", [128, NAR], F32)
    ARP = AR.ap()

    def V(off, n):
        assert off >= 0 and off + n <= NAR, (off, n)
        return ARP[:, off:off + n]

    def VB(off, n_units):
        return V(off, n_units).bitcast(BF16)

    ident = V(0, 128)
    c_rmseps = V(128, 1)
    c_lneps = V(129, 1)
    ssq = V(130, 1)
    rstd = V(131, 1)
    smean = V(132, 1)
    svar = V(133, 1)
    st1 = V(134, 1)
    ssum = V(135, 1)
    WD = V(136, 32)
    wsum = V(168, 1)
    wnorm = V(169, 1)
    dnm = V(176, 16)
    rcp = V(192, 16)
    emask4 = V(216, 16)
    emask8 = V(232, 64)
    poffv = V(296, 32)

    X2_BASE = 512
    X2 = [V(X2_BASE + i * DIM, DIM) for i in range(NT_OWN)]

    MID = X2_BASE + NT_OWN * DIM
    cos_tab = V(MID, 1024)
    sin_tab = V(MID + 1024, 1024)
    VKTh = VB(MID + 2048, 520)
    VKTl = VB(MID + 2568, 520)
    posf = V(MID + 3088, 32)
    invf = V(MID + 3120, 32)
    B = MID + 3152
    assert NAR - B >= 33000

    a1_WKVh = VB(B, 8192)
    a1_WKVl = VB(B + 8192, 8192)
    a1_bkv = V(B + 16384, 2048)
    a1_gk = V(B + 18432, 1024)
    a1_xin = [V(B + 19456, 1024), V(B + 20480, 1024)]
    a1_xTh = VB(B + 21504, 512)
    a1_xTl = VB(B + 22016, 512)
    a1_kv = V(B + 22528, 2048)
    a1_kn = V(B + 24576, 1024)
    a1_kfh = [VB(B + 25600, 512), VB(B + 26624, 512)]
    a1_kfl = [VB(B + 26112, 512), VB(B + 27136, 512)]
    a1_vph = [VB(B + 27648, 520), VB(B + 28688, 520)]
    a1_vpl = [VB(B + 28168, 520), VB(B + 29208, 520)]
    a1_scr = V(B + 29728, 1024)
    assert B + 30752 <= NAR
    spw_a = V(B + 30752, 1024)
    spw_b = V(B + 31776, 1024)
    spw_hi = VB(B + 31776, 512)
    spw_lo = VB(B + 32288, 512)
    assert B + 32800 <= NAR

    a2_WQh = VB(B, 4096)
    a2_WQl = VB(B + 4096, 4096)
    a2_WOh = VB(B + 8192, 4096)
    a2_WOl = VB(B + 12288, 4096)
    a2_bq = V(B + 16384, 1024)
    a2_gq = V(B + 17408, 1024)
    a2_bout = V(B + 18432, 1024)
    a2_gout = V(B + 19456, 1024)
    a2_xin = [V(B + 20480, 1024), V(B + 21504, 1024)]
    a2_TSh = VB(B + 22528, 512)
    a2_TSl = VB(B + 23040, 512)
    a2_b1 = V(B + 23552, 1024)
    a2_b2 = V(B + 24576, 1024)
    a2_b3 = V(B + 25600, 1024)
    a2_b4 = V(B + 26624, 1024)
    a2_scr = V(B + 27648, 1024)
    sp_in = [V(B + 28672, 1024), V(B + 29696, 1024)]
    sp_out = [V(B + 30720, 1024), V(B + 31744, 1024)]
    assert B + 32768 <= NAR

    c_ring = [VB(B + s * 4096, 4096) for s in range(4)]
    c_xTqh = VB(B + 16384, 2048)
    c_xTql = VB(B + 18432, 2048)
    c_Yq = [V(B + 20480 + j * 1024, 1024) for j in range(NT_Q)]
    c_g = [V(B + 24576 + j * 512, 512) for j in range(NT_Q)]
    c_gu = [V(B + 26624 + j * 512, 512) for j in range(NT_Q)]
    c_guTh = [VB(B + 28672, 256), VB(B + 32584, 256)]
    c_guTl = [VB(B + 28928, 256), VB(B + 32840, 256)]
    c_gwf = V(B + 29184, 64)
    c_gwh = VB(B + 29248, 32)
    c_gwl = VB(B + 29280, 32)
    c_gateb = V(B + 29312, 8)
    c_out = [V(B + 29328, 1024), V(B + 30352, 1024)]
    c_s = V(B + 31376, 8)
    c_e = V(B + 31384, 8)
    c_G = V(B + 31392, 4)
    c_m8 = V(B + 31400, 8)
    c_rank = V(B + 31408, 8)
    c_sel = V(B + 31416, 8)
    c_sw = V(B + 31424, 8)
    c_scr1 = V(B + 31432, 64)
    c_scr2 = V(B + 31496, 64)
    c_scrA = V(B + 31560, 1024)
    assert B + 33096 <= NAR

    s_identi = V(B + 26624, 128).bitcast(I32)
    s_posi = V(B + 27648, 32).bitcast(I32)
    s_ki = V(B + 25600, 1024).bitcast(I32)
    s_theta = a1_kn
    s_fold = a1_scr
    s_kf = a1_kv[:, 0:1024]
    s_emi4 = a1_kv[:, 1024:1040].bitcast(I32)
    s_emi8 = a1_kv[:, 1040:1104].bitcast(I32)

    PS = [nc.alloc_psum_tensor(f"ps{i}", [128, 512], F32).ap() for i in range(4)]
    PDD = [nc.alloc_psum_tensor(f"pd{i}", [128, 1024], F32).ap() for i in range(2)]

    def vk_bank(h, rows=128):
        g, k = divmod(h, 4)
        return PDD[g // 2][0:rows, (g % 2) * 512 + k * 65:(g % 2) * 512 + k * 65 + 65]

    p = Prog(nc)
    for s in ("sP", "sPG", "sX0", "sX1", "sWA", "sW0", "sW1", "sW2", "sW3",
              "sQ0", "sQ1", "sQ2", "sQ3",
              "sVD", "sY0", "sY1", "sO", "sWS0", "sWS1", "sWB0", "sWB1",
              "sT", "sV", "sS", "sG"):
        p.sem(s)

    def dma(eng, out, in_, sem="sP"):
        h = p.sems[sem]
        p.inc(sem, 16)
        p.on(eng, lambda e, out=out, in_=in_, h=h:
             e.dma_start(out=out, in_=in_).then_inc(h, 16))
        return p.cnt[sem]

    def t_step(ins_fn, inc=True):
        if inc:
            p.on("tensor", p.tag(ins_fn, "sT"))
        else:
            p.on("tensor", ins_fn)
        return p.cnt["sT"]

    def v_step(ins_fn, inc=True):
        p.on("vector", lambda e: e.drain())
        if inc:
            p.on("vector", p.tag(ins_fn, "sV"))
        else:
            p.on("vector", ins_fn)
        return p.cnt["sV"]

    def s_step(ins_fn, inc=True):
        p.on("scalar", lambda e: e.drain())
        if inc:
            p.on("scalar", p.tag(ins_fn, "sS"))
        else:
            p.on("scalar", ins_fn)
        return p.cnt["sS"]

    def g_step(ins_fn, inc=True):
        p.on("gpsimd", lambda e: e.drain())
        if inc:
            p.on("gpsimd", p.tag(ins_fn, "sG"))
        else:
            p.on("gpsimd", ins_fn)
        return p.cnt["sG"]

    def split_v(hi, lo, src, inc=True):
        v_step(lambda e: e.tensor_copy(hi, src), inc=False)
        return v_step(lambda e: e.tensor_tensor(lo, src, hi, op=OP.subtract), inc=inc)

    def split_g(hi, lo, src, inc=True):
        g_step(lambda e: e.tensor_copy(hi, src), inc=False)
        return g_step(lambda e: e.tensor_tensor(lo, src, hi, op=OP.subtract), inc=inc)

    marks = {"ps01": 0}

    def transpose8s(src, dsth, dstl, n=8, src_mark=0):
        assert n % 2 == 0
        p.wait("tensor", "sV", max(marks["ps01"], src_mark))
        for kc in range(n):
            bank = PS[kc % 2]
            off = (kc // 2) * 128
            t_step(lambda e, bank=bank, off=off, kc=kc, src=src:
                   e.transpose(bank[:, off:off + 128],
                               src[:, kc * 128:(kc + 1) * 128], ident),
                   inc=(kc == n - 1))
        p.wait("vector", "sT")
        nb = (n // 2) * 128
        for b in range(2):
            hv = dsth.rearrange("p (a two c) -> p a two c", two=2, c=128)[:, :, b, :]
            lv = dstl.rearrange("p (a two c) -> p a two c", two=2, c=128)[:, :, b, :]
            sv = PS[b][:, 0:nb].rearrange("p (a c) -> p a c", c=128)
            split_v(hv, lv, sv, inc=(b == 1))
        marks["ps01"] = p.cnt["sV"]
        return marks["ps01"]

    def mm3(bank, sh, sl, mh, ml, start, stop, inc=False):
        t_step(lambda e: e.matmul(bank, sh, mh, start=start, stop=False,
                                  skip_group_check=True), inc=False)
        t_step(lambda e: e.matmul(bank, sh, ml, start=False, stop=False,
                                  skip_group_check=True), inc=False)
        return t_step(lambda e: e.matmul(bank, sl, mh, start=False, stop=stop,
                                         skip_group_check=True), inc=inc)

    def w_chunk(kind, e, cc):
        if kind == "g":
            t = wg[e] if e < 8 else swg[:, (e - 8) * 512:(e - 7) * 512]
        elif kind == "u":
            t = wu[e] if e < 8 else swu[:, (e - 8) * 512:(e - 7) * 512]
        else:
            t = wd[e] if e < 8 else swd[(e - 8) * 512:(e - 7) * 512, :]
        if kind in ("g", "u"):
            return t.rearrange("(a p) n -> p a n", p=128)[:, 2 * cc:2 * cc + 2, :]
        return t.rearrange("(a p) n -> p a n", p=128)[:, cc:cc + 1, :]

    slot_key = [(e0, k) for e0 in range(NE) for k in ("g", "u", "d")]
    sp_jobs = [(si, cc) for si in range(NSLOT) for cc in range(4)]
    wsp_wbs = [dict() for _ in range(NSLOT)]

    dx_bkv = dma("sync", a1_bkv[:], bqkv[0:1, DIM:3 * DIM].broadcast_to([128, 2048]))
    dx_gk = dma("sync", a1_gk[:], gkv[0:1, :].broadcast_to([128, DIM]))
    dx_xin = {}
    dx_xin[0] = dma("sync", a1_xin[0][:], xb[0:128, :], sem="sX0")
    dx_poff = dma("sync", poffv, posoff[0:1, :].broadcast_to([128, NT_ALL]))

    g_step(lambda e: e.iota(s_identi, pattern=[[-1, 128]], base=0, channel_multiplier=1))
    p.wait("vector", "sG")
    v_step(lambda e: e.tensor_scalar(ident, s_identi, 0, None, op0=OP.is_equal))
    g_step(lambda e: e.iota(s_emi4, pattern=[[1, 4], [-1, 4]], base=0, channel_multiplier=0))
    g_step(lambda e: e.iota(s_emi8, pattern=[[1, 8], [-1, 8]], base=0, channel_multiplier=0))
    p.wait("vector", "sG")
    v_step(lambda e: e.tensor_scalar(emask4, s_emi4, 0, None, op0=OP.is_gt), inc=False)
    v_step(lambda e: e.tensor_scalar(emask8, s_emi8, 0, None, op0=OP.is_gt), inc=False)
    v_step(lambda e: e.memset(VKTh[:, :], 0.0), inc=False)
    v_step(lambda e: e.memset(VKTl[:, :], 0.0), inc=False)
    v_step(lambda e: e.memset(c_rmseps, RMS_EPS), inc=False)
    v_step(lambda e: e.memset(c_lneps, LN_EPS), inc=False)
    for bi in range(2):
        v_step(lambda e, bi=bi: e.memset(
            a1_vph[bi].rearrange("p (h d) -> p h d", d=65)[:, :, 64:65], 1.0),
            inc=False)
        v_step(lambda e, bi=bi: e.memset(a1_vpl[bi][:, :], 0.0), inc=False)

    g_step(lambda e: e.iota(s_posi, pattern=[[128, NT_ALL]], base=0, channel_multiplier=1))
    p.wait("vector", "sG")
    v_step(lambda e: e.tensor_copy(posf, s_posi), inc=False)
    p.wait("vector", "sP")
    v_step(lambda e: e.tensor_tensor(posf, posf, poffv, op=OP.add))
    for i_ in range(32):
        val = float(np.float32(1.0 / THETA ** (i_ / 32.0)))
        v_step(lambda e, i_=i_, val=val: e.memset(invf[:, i_:i_ + 1], val),
               inc=(i_ == 31))
    v_step(lambda e: e.tensor_tensor(
        s_theta[:].rearrange("p (t i) -> p t i", i=32),
        posf.unsqueeze(2).broadcast_to([128, 32, 32]),
        invf.unsqueeze(1).broadcast_to([128, 32, 32]), op=OP.mult))

    def sincos(dst, shift):
        if shift:
            v_step(lambda e: e.tensor_scalar(s_fold, s_theta, shift, None, op0=OP.add), inc=False)
            src = s_fold
        else:
            src = s_theta
        v_step(lambda e, src=src: e.tensor_scalar(dst, src, 1.0 / TWO_PI, None,
                                                  op0=OP.mult), inc=False)
        v_step(lambda e: e.tensor_copy(s_ki, dst), inc=False)
        v_step(lambda e: e.tensor_copy(s_kf, s_ki), inc=False)
        for ci, c in enumerate((CW1, CW2, CW3)):
            v_step(lambda e, c=c, src=src: e.scalar_tensor_tensor(
                dst, s_kf, -c, src if c == CW1 else dst,
                op0=OP.mult, op1=OP.add), inc=False)
        v_step(lambda e: e.tensor_scalar(s_kf, dst, math.pi, None, op0=OP.is_gt), inc=False)
        v_step(lambda e: e.scalar_tensor_tensor(dst, s_kf, -TWO_PI, dst,
                                                op0=OP.mult, op1=OP.add), inc=False)
        v_step(lambda e: e.tensor_scalar(s_kf, dst, -math.pi, None, op0=OP.is_lt), inc=False)
        v_step(lambda e: e.scalar_tensor_tensor(dst, s_kf, TWO_PI, dst,
                                                op0=OP.mult, op1=OP.add), inc=False)
        v_step(lambda e: e.tensor_scalar(dst, dst, math.pi, None, op0=OP.min), inc=False)
        v_step(lambda e: e.tensor_scalar(dst, dst, -math.pi, None, op0=OP.max))
        p.wait("scalar", "sV")
        s_step(lambda e: e.activation(dst, dst, AF.Sin))

    sincos(sin_tab, 0.0)
    sincos(cos_tab, math.pi / 2.0)
    p.wait("vector", "sS")
    marks["ps01"] = p.cnt["sV"]

    wkv_v = wkvT[:].rearrange("(a p) (b n) -> p a b n", p=128, n=1024)
    dwk = {}
    for c in range(16):
        stg = spw_a if c % 2 == 0 else spw_b
        dwk[c] = dma("gpsimd", stg, wkv_v[:, c // 2, c % 2, :],
                     sem=f"sWS{c % 2}")
        p.wait("gpsimd", f"sWS{c % 2}", dwk[c])
        split_g(a1_WKVh[:, c * 1024:(c + 1) * 1024],
                a1_WKVl[:, c * 1024:(c + 1) * 1024], stg)
    wkv_mark = p.cnt["sG"]

    wq_v = wqT[:].rearrange("(a p) n -> p a n", p=128)
    wo_v = woutT[:].rearrange("(a p) n -> p a n", p=128)
    wq_out_done = 0
    for c in range(16):
        src = wq_v[:, c, :] if c < 8 else wo_v[:, c - 8, :]
        dk = dma("gpsimd", spw_a, src, sem="sWS0")
        p.wait("gpsimd", "sWS0", dk)
        if wq_out_done:
            p.wait("gpsimd", "sWA", wq_out_done)
        gmark = split_g(spw_hi, spw_lo, spw_a)
        p.wait("gpsimd", "sG", gmark)
        dma("gpsimd", wspq[2 * (c // 8), :, (c % 8) * 1024:(c % 8 + 1) * 1024],
            spw_hi, sem="sWA")
        wq_out_done = dma(
            "gpsimd", wspq[2 * (c // 8) + 1, :, (c % 8) * 1024:(c % 8 + 1) * 1024],
            spw_lo, sem="sWA")

    SP_START = 120
    spw_pair = V(B + 31776, 1024).bitcast(BF16)
    a1_wb_last = {}
    for ji in range(SP_START):
        si, cc = sp_jobs[ji]
        e0, kind = slot_key[si]
        nsub = 512 if kind in ("g", "u") else 1024
        dk = dma("gpsimd", spw_a.rearrange("p (a n) -> p a n", n=nsub),
                 w_chunk(kind, e0, cc), sem="sWS0")
        p.wait("gpsimd", "sWS0", dk)
        if ji == 0:
            p.wait("gpsimd", "sWA", wq_out_done)
        else:
            par_p = (ji - 1) % 2
            p.wait("gpsimd", f"sWB{par_p}", a1_wb_last[par_p])
        gmark = split_g(spw_hi, spw_lo, spw_a)
        p.wait("gpsimd", "sG", gmark)
        a1_wb_last[ji % 2] = dma("gpsimd", wsp[si, :, cc * 2048:(cc + 1) * 2048],
                                 spw_pair, sem=f"sWB{ji % 2}")
        wsp_wbs[si][f"sWB{ji % 2}"] = a1_wb_last[ji % 2]

    mark_tr = {}
    mark_vk = {}
    bank_mark = {2: 0, 3: 0}
    kfvp_mark = {0: 0, 1: 0}

    p.wait("tensor", "sX0", dx_xin[0])
    ts_marks = {0: transpose8s(a1_xin[0], a1_xTh, a1_xTl)}
    mark_tr[0] = p.cnt["sT"]
    for t in range(NT_ALL):
        if t + 1 < NT_ALL:
            if t - 1 >= 0:
                p.wait("sync", "sT", mark_tr[t - 1])
            dx_xin[t + 1] = dma("sync", a1_xin[(t + 1) % 2][:],
                                xb[(t + 1) * 128:(t + 2) * 128, :],
                                sem=f"sX{(t + 1) % 2}")
        ts_mark = ts_marks[t]
        if t == 0:
            p.wait("tensor", "sG", wkv_mark)
        for oc in range(4):
            bank = PS[2 + oc % 2]
            p.wait("tensor", "sV", max(bank_mark[2 + oc % 2], ts_mark))
            for kc in range(8):
                mm3(bank,
                    a1_xTh[:, kc * 128:(kc + 1) * 128],
                    a1_xTl[:, kc * 128:(kc + 1) * 128],
                    a1_WKVh[:, kc * 2048 + oc * 512:kc * 2048 + (oc + 1) * 512],
                    a1_WKVl[:, kc * 2048 + oc * 512:kc * 2048 + (oc + 1) * 512],
                    start=(kc == 0), stop=(kc == 7), inc=(kc == 7))
            p.wait("vector", "sT")
            if t == 0 and oc == 0:
                p.wait("vector", "sP")
            bank_mark[2 + oc % 2] = v_step(lambda e, bank=bank, oc=oc: e.tensor_tensor(
                a1_kv[:, oc * 512:(oc + 1) * 512], bank,
                a1_bkv[:, oc * 512:(oc + 1) * 512], op=OP.add))
        if t + 1 < NT_ALL:
            p.wait("tensor", f"sX{(t + 1) % 2}", dx_xin[t + 1])
            ts_marks[t + 1] = transpose8s(a1_xin[(t + 1) % 2], a1_xTh, a1_xTl)
            mark_tr[t + 1] = p.cnt["sT"]
        if t >= 1:
            tt = t - 1
            p.wait("tensor", "sV", kfvp_mark[tt % 2])
            for h in range(NH):
                mm3(vk_bank(h, 64),
                    a1_kfh[tt % 2][:, h * 64:(h + 1) * 64],
                    a1_kfl[tt % 2][:, h * 64:(h + 1) * 64],
                    a1_vph[tt % 2][:, h * 65:(h + 1) * 65],
                    a1_vpl[tt % 2][:, h * 65:(h + 1) * 65],
                    start=(tt == 0 and h % 4 == 0), stop=False,
                    inc=(h == NH - 1))
            mark_vk[tt] = p.cnt["sT"]
        p.wait("scalar", "sV")
        s_step(lambda e: e.activation(a1_scr, a1_kv[:, 0:1024], AF.Square,
                                      accum_out=ssq), inc=False)
        s_step(lambda e: e.activation(st1, ssq, AF.Ln, scale=1.0 / DIM,
                                      bias=c_rmseps), inc=False)
        s_step(lambda e: e.activation(rstd, st1, AF.Exp, scale=-0.5))
        p.wait("vector", "sS")
        if t == 0:
            p.wait("vector", "sP")
        v_step(lambda e: e.scalar_tensor_tensor(a1_kn, a1_kv[:, 0:1024], rstd,
                                                a1_gk, op0=OP.mult, op1=OP.mult), inc=False)
        cs = cos_tab[:, t * 32:(t + 1) * 32].unsqueeze(1).broadcast_to([128, 16, 32])
        sn = sin_tab[:, t * 32:(t + 1) * 32].unsqueeze(1).broadcast_to([128, 16, 32])
        knv = a1_kn[:].rearrange("p (h d two) -> p h d two", h=16, two=2)
        kro = a1_kv[:, 0:1024]
        krv = kro.rearrange("p (h d two) -> p h d two", h=16, two=2)
        tr, ti = knv[:, :, :, 0], knv[:, :, :, 1]
        sc1 = a1_scr[:, 0:512].rearrange("p (h d) -> p h d", d=32)
        sc2 = a1_scr[:, 512:1024].rearrange("p (h d) -> p h d", d=32)
        v_step(lambda e, tr=tr, cs=cs, sc1=sc1: e.tensor_tensor(sc1, tr, cs, op=OP.mult), inc=False)
        v_step(lambda e, ti=ti, sn=sn, sc2=sc2: e.tensor_tensor(sc2, ti, sn, op=OP.mult), inc=False)
        v_step(lambda e, krv=krv, sc1=sc1, sc2=sc2: e.tensor_tensor(krv[:, :, :, 0], sc1, sc2, op=OP.subtract), inc=False)
        v_step(lambda e, tr=tr, sn=sn, sc1=sc1: e.tensor_tensor(sc1, tr, sn, op=OP.mult), inc=False)
        v_step(lambda e, ti=ti, cs=cs, sc2=sc2: e.tensor_tensor(sc2, ti, cs, op=OP.mult), inc=False)
        v_step(lambda e, krv=krv, sc1=sc1, sc2=sc2: e.tensor_tensor(krv[:, :, :, 1], sc1, sc2, op=OP.add), inc=False)
        v_step(lambda e, kro=kro: e.tensor_scalar(kro, kro, 0.0, None, op0=OP.max), inc=False)
        split_v(a1_kfh[t % 2][:, :], a1_kfl[t % 2][:, :], kro, inc=False)
        vsrc = a1_kv[:, 1024:2048].rearrange("p (h d) -> p h d", d=64)
        vph = a1_vph[t % 2].rearrange("p (h d) -> p h d", d=65)[:, :, 0:64]
        vpl = a1_vpl[t % 2].rearrange("p (h d) -> p h d", d=65)[:, :, 0:64]
        v_step(lambda e, vph=vph, vsrc=vsrc: e.tensor_copy(vph, vsrc), inc=False)
        kfvp_mark[t % 2] = v_step(lambda e, vpl=vpl, vsrc=vsrc, vph=vph: e.tensor_tensor(vpl, vsrc, vph, op=OP.subtract))

    tt = NT_ALL - 1
    p.wait("tensor", "sV", kfvp_mark[tt % 2])
    for h in range(NH):
        mm3(vk_bank(h, 64),
            a1_kfh[tt % 2][:, h * 64:(h + 1) * 64],
            a1_kfl[tt % 2][:, h * 64:(h + 1) * 64],
            a1_vph[tt % 2][:, h * 65:(h + 1) * 65],
            a1_vpl[tt % 2][:, h * 65:(h + 1) * 65],
            start=False, stop=True, inc=(h == NH - 1))
    mark_vk[tt] = p.cnt["sT"]

    VKS = a1_kv[0:64, 0:1040]
    p.wait("vector", "sT")
    for g in range(4):
        src = PDD[g // 2][:, (g % 2) * 512:(g % 2) * 512 + 260]
        v_step(lambda e, g=g, src=src: e.tensor_copy(
            a1_kv[0:64, g * 260:(g + 1) * 260], src[0:64, :]), inc=False)
    hi_all = a1_kn[0:64, 0:520].bitcast(BF16)
    lo_all = a1_scr[0:64, 0:520].bitcast(BF16)
    split_v(hi_all, lo_all, VKS, inc=False)
    evens = lambda ap: ap.rearrange("p (h two d) -> p h two d", two=2, d=65)[:, :, 0, :]
    odds = lambda ap: ap.rearrange("p (h two d) -> p h two d", two=2, d=65)[:, :, 1, :]
    v_step(lambda e: e.tensor_copy(evens(VKTh[0:64, :]), evens(hi_all)), inc=False)
    v_step(lambda e: e.tensor_copy(evens(VKTl[0:64, :]), evens(lo_all)))
    p.wait("sync", "sV")
    dma("sync", odds(VKTh[64:128, :]), odds(hi_all), sem="sVD")
    dx_vkt = dma("sync", odds(VKTl[64:128, :]), odds(lo_all), sem="sVD")
    if t_vkt is not None:
        p.wait("sync", "sVD")
        dma("sync", t_vkt[:, 0:520], VKTh[:, :].bitcast(F32), sem="sO")
        dma("sync", t_vkt[:, 520:1040], VKTl[:, :].bitcast(F32), sem="sO")
    if t_kf0 is not None:
        dma("sync", t_kf0[:, 0:512], a1_kfh[tt % 2][:, :].bitcast(F32), sem="sO")
        dma("sync", t_kf0[:, 512:1024], a1_kfl[tt % 2][:, :].bitcast(F32), sem="sO")
    if t_vkt is not None or t_kf0 is not None:
        p.wait("vector", "sO")

    if phases == 1:
        for s in ("sY0", "sY1", "sO"):
            p.on("sync", lambda e, s=s: e.wait_ge(p.sems[s], p.cnt[s]) if p.cnt[s] else None)
        p.replay()
        return nc, tap_t

    p.wait("gpsimd", "sT", mark_vk[NT_ALL - 1])
    p.wait("gpsimd", "sV")
    p.wait("gpsimd", "sVD")
    p.wait("gpsimd", "sWA", wq_out_done)
    p.wait("sync", "sT", mark_vk[NT_ALL - 1])
    p.wait("sync", "sV")
    dx_bq = dma("sync", a2_bq[:], bqkv[0:1, 0:DIM].broadcast_to([128, DIM]))
    dx_gq = dma("sync", a2_gq[:], gqv[0:1, :].broadcast_to([128, DIM]))
    dx_bout = dma("sync", a2_bout[:], boutv[0:1, :].broadcast_to([128, DIM]))
    dx_gout = dma("sync", a2_gout[:], goutv[0:1, :].broadcast_to([128, DIM]))

    p.wait("sync", "sWA", wq_out_done)
    dq_wq = [dma("sync", a2_WQh[:, :], wspq[0], sem="sQ0"),
             dma("sync", a2_WQl[:, :], wspq[1], sem="sQ1"),
             dma("sync", a2_WOh[:, :], wspq[2], sem="sQ2"),
             dma("sync", a2_WOl[:, :], wspq[3], sem="sQ3")]

    sp_state = {"i": SP_START}
    sp_wb = dict(a1_wb_last)
    sp_dk = {}

    def sp_in_dma(ji):
        si, cc = sp_jobs[ji]
        e0, kind = slot_key[si]
        nsub = 512 if kind in ("g", "u") else 1024
        sp_dk[ji] = dma("gpsimd",
                        sp_in[ji % 2].rearrange("p (a n) -> p a n", n=nsub),
                        w_chunk(kind, e0, cc), sem=f"sWS{ji % 2}")

    def emit_split_chunks(nmax):
        for _ in range(nmax):
            if sp_state["i"] >= len(sp_jobs):
                return
            ji = sp_state["i"]
            sp_state["i"] += 1
            si, cc = sp_jobs[ji]
            par = ji % 2
            if ji not in sp_dk:
                sp_in_dma(ji)
            if ji + 1 < len(sp_jobs):
                sp_in_dma(ji + 1)
            p.wait("gpsimd", f"sWS{par}", sp_dk[ji])
            if sp_wb.get(par, 0):
                p.wait("gpsimd", f"sWB{par}", sp_wb[par])
            gmark = split_g(sp_out[par][:, 0:512].bitcast(BF16),
                            sp_out[par][:, 512:1024].bitcast(BF16), sp_in[par])
            p.wait("gpsimd", "sG", gmark)
            sp_wb[par] = dma("gpsimd", wsp[si, :, cc * 2048:(cc + 1) * 2048],
                             sp_out[par][:, :].bitcast(BF16), sem=f"sWB{par}")
            wsp_wbs[si][f"sWB{par}"] = sp_wb[par]

    dx_xin2 = {}
    dx_xin2[0] = dma("sync", a2_xin[0][:], xb[0:128, :], sem="sX0")
    mark_tr2 = {}
    mark_res2 = {}
    mark_x2 = {}
    div_mark = 0

    for i in range(NT_OWN):
        if i + 1 < NT_OWN:
            if i - 1 >= 0:
                p.wait("sync", "sV", mark_res2[i - 1])
            dx_xin2[i + 1] = dma("sync", a2_xin[(i + 1) % 2][:],
                                 xb[(i + 1) * 128:(i + 2) * 128, :],
                                 sem=f"sX{(i + 1) % 2}")
        if i == 0:
            p.wait("tensor", "sX0", dx_xin2[0])
            ts_marks2 = {0: transpose8s(a2_xin[0], a2_TSh, a2_TSl)}
            p.wait("tensor", "sVD", dx_vkt)
            for si_, dv_ in enumerate(dq_wq):
                p.wait("tensor", f"sQ{si_}", dv_)
        ts_mark = ts_marks2[i]
        for oc in range(2):
            bank = PS[2 + oc]
            p.wait("tensor", "sV", max(bank_mark[2 + oc], ts_mark))
            for kc in range(8):
                mm3(bank,
                    a2_TSh[:, kc * 128:(kc + 1) * 128],
                    a2_TSl[:, kc * 128:(kc + 1) * 128],
                    a2_WQh[:, kc * 1024 + oc * 512:kc * 1024 + (oc + 1) * 512],
                    a2_WQl[:, kc * 1024 + oc * 512:kc * 1024 + (oc + 1) * 512],
                    start=(kc == 0), stop=(kc == 7), inc=(kc == 7))
        p.wait("vector", "sT")
        if i == 0:
            p.wait("vector", "sP")
        for oc in range(2):
            bank_mark[2 + oc] = v_step(lambda e, oc=oc: e.tensor_tensor(
                a2_b1[:, oc * 512:(oc + 1) * 512], PS[2 + oc],
                a2_bq[:, oc * 512:(oc + 1) * 512], op=OP.add), inc=True)
        p.wait("scalar", "sV")
        s_step(lambda e: e.activation(a2_scr, a2_b1, AF.Square, accum_out=ssq), inc=False)
        s_step(lambda e: e.activation(st1, ssq, AF.Ln, scale=1.0 / DIM, bias=c_rmseps), inc=False)
        s_step(lambda e: e.activation(rstd, st1, AF.Exp, scale=-0.5))
        p.wait("vector", "sS")
        v_step(lambda e: e.scalar_tensor_tensor(a2_b2, a2_b1, rstd, a2_gq,
                                                op0=OP.mult, op1=OP.mult), inc=False)
        cs = cos_tab[:, i * 32:(i + 1) * 32].unsqueeze(1).broadcast_to([128, 16, 32])
        sn = sin_tab[:, i * 32:(i + 1) * 32].unsqueeze(1).broadcast_to([128, 16, 32])
        qnv = a2_b2[:].rearrange("p (h d two) -> p h d two", h=16, two=2)
        qfv = a2_b3[:].rearrange("p (h d two) -> p h d two", h=16, two=2)
        tr, ti = qnv[:, :, :, 0], qnv[:, :, :, 1]
        sc1 = a2_scr[:, 0:512].rearrange("p (h d) -> p h d", d=32)
        sc2 = a2_scr[:, 512:1024].rearrange("p (h d) -> p h d", d=32)
        v_step(lambda e, tr=tr, cs=cs, sc1=sc1: e.tensor_tensor(sc1, tr, cs, op=OP.mult), inc=False)
        v_step(lambda e, ti=ti, sn=sn, sc2=sc2: e.tensor_tensor(sc2, ti, sn, op=OP.mult), inc=False)
        v_step(lambda e, qfv=qfv, sc1=sc1, sc2=sc2: e.tensor_tensor(qfv[:, :, :, 0], sc1, sc2, op=OP.subtract), inc=False)
        v_step(lambda e, tr=tr, sn=sn, sc1=sc1: e.tensor_tensor(sc1, tr, sn, op=OP.mult), inc=False)
        v_step(lambda e, ti=ti, cs=cs, sc2=sc2: e.tensor_tensor(sc2, ti, cs, op=OP.mult), inc=False)
        v_step(lambda e, qfv=qfv, sc1=sc1, sc2=sc2: e.tensor_tensor(qfv[:, :, :, 1], sc1, sc2, op=OP.add), inc=False)
        qf_mark = v_step(lambda e: e.tensor_scalar(a2_b3[:], a2_b3[:], 0.0, None, op0=OP.max))
        if i == 0 and t_qf0 is not None:
            p.wait("sync", "sV")
            dma("sync", t_qf0[:], a2_b3[:], sem="sO")
            p.wait("vector", "sO")
        ts_mark = transpose8s(a2_b3, a2_TSh, a2_TSl, src_mark=qf_mark)
        p.wait("tensor", "sV", max(div_mark, ts_mark))
        for h in range(NH):
            mm3(vk_bank(h),
                a2_TSh[:, 128 * (h // 2):128 * (h // 2) + 128],
                a2_TSl[:, 128 * (h // 2):128 * (h // 2) + 128],
                VKTh[:, 65 * h:65 * h + 65],
                VKTl[:, 65 * h:65 * h + 65],
                start=True, stop=True, inc=(h == NH - 1))
        p.wait("vector", "sT")
        for g in range(4):
            src = PDD[g // 2][:, (g % 2) * 512:(g % 2) * 512 + 512]
            v_step(lambda e, g=g, src=src: e.tensor_scalar(
                dnm[:, g * 4:(g + 1) * 4],
                src[:, 0:260].rearrange("p (h d) -> p h d", d=65)[:, :, 64],
                EPS_LIN, None, op0=OP.add), inc=False)
        v_step(lambda e: e.reciprocal(rcp, dnm), inc=False)
        for g in range(4):
            src = PDD[g // 2][:, (g % 2) * 512:(g % 2) * 512 + 512]
            v_step(lambda e, g=g, src=src: e.tensor_tensor(
                a2_b4[:, g * 256:(g + 1) * 256].rearrange("p (h d) -> p h d", d=64),
                src[:, 0:260].rearrange("p (h d) -> p h d", d=65)[:, :, 0:64],
                rcp[:, g * 4:(g + 1) * 4].unsqueeze(2).broadcast_to([128, 4, 64]),
                op=OP.mult), inc=(g == 3))
        div_mark = p.cnt["sV"]
        if i == 0 and t_attn0 is not None:
            p.wait("sync", "sV")
            dma("sync", t_attn0[:], a2_b4[:], sem="sO")
            p.wait("vector", "sO")
        ts_mark = transpose8s(a2_b4, a2_TSh, a2_TSl, src_mark=div_mark)
        for oc in range(2):
            bank = PS[2 + oc]
            p.wait("tensor", "sV", max(bank_mark[2 + oc], ts_mark))
            for kc in range(8):
                mm3(bank,
                    a2_TSh[:, kc * 128:(kc + 1) * 128],
                    a2_TSl[:, kc * 128:(kc + 1) * 128],
                    a2_WOh[:, kc * 1024 + oc * 512:kc * 1024 + (oc + 1) * 512],
                    a2_WOl[:, kc * 1024 + oc * 512:kc * 1024 + (oc + 1) * 512],
                    start=(kc == 0), stop=(kc == 7), inc=(kc == 7))
        if i + 1 < NT_OWN:
            p.wait("tensor", f"sX{(i + 1) % 2}", dx_xin2[i + 1])
            ts_marks2[i + 1] = transpose8s(a2_xin[(i + 1) % 2], a2_TSh, a2_TSl)
        p.wait("vector", "sT")
        for oc in range(2):
            bank_mark[2 + oc] = v_step(lambda e, oc=oc: e.tensor_tensor(
                a2_b1[:, oc * 512:(oc + 1) * 512], PS[2 + oc],
                a2_bout[:, oc * 512:(oc + 1) * 512], op=OP.add), inc=True)
        p.wait("scalar", "sV")
        s_step(lambda e: e.activation(a2_scr, a2_b1, AF.Square, accum_out=ssq), inc=False)
        s_step(lambda e: e.activation(st1, ssq, AF.Ln, scale=1.0 / DIM, bias=c_rmseps), inc=False)
        s_step(lambda e: e.activation(rstd, st1, AF.Exp, scale=-0.5))
        p.wait("vector", "sS")
        v_step(lambda e: e.scalar_tensor_tensor(a2_b2, a2_b1, rstd, a2_gout,
                                                op0=OP.mult, op1=OP.mult), inc=False)
        v_step(lambda e, i=i: e.tensor_tensor(a2_b3[:], a2_b2, a2_xin[i % 2][:], op=OP.add))
        mark_res2[i] = p.cnt["sV"]
        p.wait("scalar", "sV")
        s_step(lambda e: e.activation(a2_scr, a2_b3[:], AF.Copy, accum_out=ssum), inc=False)
        s_step(lambda e: e.activation(a2_scr, a2_b3[:], AF.Square, accum_out=ssq))
        p.wait("vector", "sS")
        v_step(lambda e: e.tensor_scalar(smean, ssum, 1.0 / DIM, None, op0=OP.mult), inc=False)
        v_step(lambda e: e.tensor_tensor(st1, smean, smean, op=OP.mult), inc=False)
        v_step(lambda e: e.scalar_tensor_tensor(svar, ssq, 1.0 / DIM, st1,
                                                op0=OP.mult, op1=OP.subtract))
        p.wait("scalar", "sV")
        s_step(lambda e: e.activation(st1, svar, AF.Ln, bias=c_lneps), inc=False)
        s_step(lambda e: e.activation(rstd, st1, AF.Exp, scale=-0.5))
        p.wait("vector", "sS")
        mark_x2[i] = v_step(lambda e, i=i: e.tensor_scalar(X2[i], a2_b3[:], smean, rstd,
                                              op0=OP.subtract, op1=OP.mult))
        if phases == 3:
            emit_split_chunks(8)

    if phases == 3:
        emit_split_chunks(len(sp_jobs))

    if t_x2 is not None:
        p.wait("sync", "sV")
        for i in range(NT_OWN):
            dma("sync", t_x2[i * 128:(i + 1) * 128, :], X2[i], sem="sO")
        p.wait("vector", "sO")

    if phases == 2:
        for s in ("sY0", "sY1", "sO"):
            p.on("sync", lambda e, s=s: e.wait_ge(p.sems[s], p.cnt[s]) if p.cnt[s] else None)
        p.replay()
        return nc, tap_t

    p.wait("gpsimd", "sT")
    p.wait("gpsimd", "sV")
    dx_gw = dma("gpsimd", c_gwf[:].rearrange("p (c n) -> p c n", n=8),
                gatew[:].rearrange("(c p) n -> p c n", p=128), sem="sPG")
    dx_gateb = dma("gpsimd", c_gateb[:], gateb[0:1, :].broadcast_to([128, 8]), sem="sPG")
    p.wait("vector", "sPG")
    p.wait("vector", "sWB0")
    p.wait("vector", "sWB1")
    p.wait("scalar", "sWB0")
    p.wait("scalar", "sWB1")
    gw_split_v = split_v(c_gwh[:, :], c_gwl[:, :], c_gwf[:])

    slot_mark = [0, 0, 0, 0]
    load_val = {}
    load_q = [(qq, si) for qq in range(NQ) for si in range(NSLOT)]
    slot_of = {}
    qstate = {"qi": 0}
    ring_started = {"done": False}

    def emit_next_load(s):
        if qstate["qi"] >= len(load_q):
            return
        key = load_q[qstate["qi"]]
        qstate["qi"] += 1
        p.wait("gpsimd", "sT", slot_mark[s])
        _, si = key
        for semn, val in wsp_wbs[si].items():
            p.wait("gpsimd", semn, val)
        load_val[key] = dma("gpsimd", c_ring[s][:, :], wsp[si], sem=f"sW{s}")
        slot_of[key] = s

    for s in range(4):
        emit_next_load(s)

    def free_slot(qq, si):
        s = slot_of[(qq, si)]
        slot_mark[s] = p.cnt["sT"]
        emit_next_load(s)

    def ring_views(slot, kind):
        sb = c_ring[slot]
        if kind in ("g", "u"):
            def wh(c):
                cc, off = divmod(c, 2)
                return sb[:, cc * 2048 + off * 512:cc * 2048 + off * 512 + 512]
            def wl(c):
                cc, off = divmod(c, 2)
                return sb[:, cc * 2048 + 1024 + off * 512:cc * 2048 + 1024 + off * 512 + 512]
        else:
            def wh(c, oc):
                return sb[:, c * 2048 + oc * 512:c * 2048 + oc * 512 + 512]
            def wl(c, oc):
                return sb[:, c * 2048 + 1024 + oc * 512:c * 2048 + 1024 + oc * 512 + 512]
        return wh, wl

    out_dma = {}
    s_markPS = {2: 0, 3: 0}
    v_markPS = {2: 0, 3: 0}
    yq_mark = {0: 0, 1: 0}
    gu_mark = {0: 0, 1: 0}
    def prep_quarter(q):
        tiles = [q * NT_Q + j for j in range(NT_Q)]
        for j, ti_ in enumerate(tiles):
            xtq_mark = transpose8s(X2[ti_], c_xTqh[:, j * 1024:(j + 1) * 1024],
                                   c_xTql[:, j * 1024:(j + 1) * 1024],
                                   src_mark=mark_x2.get(ti_, 0))
        p.wait("tensor", "sPG")
        p.wait("tensor", "sV", xtq_mark)
        for j in range(NT_Q):
            gb = 2 + j % 2
            p.wait("tensor", "sS", s_markPS[gb])
            p.wait("tensor", "sV", v_markPS[gb])
            for kc in range(8):
                mm3(PS[gb][:, 0:8],
                    c_xTqh[:, j * 1024 + kc * 128:j * 1024 + (kc + 1) * 128],
                    c_xTql[:, j * 1024 + kc * 128:j * 1024 + (kc + 1) * 128],
                    c_gwh[:, kc * 8:(kc + 1) * 8],
                    c_gwl[:, kc * 8:(kc + 1) * 8],
                    start=(kc == 0), stop=(kc == 7), inc=(kc == 7))
            p.wait("scalar", "sT")
            p.wait("scalar", "sV")
            s_markPS[gb] = s_step(lambda e, gb=gb: e.activation(c_e, PS[gb][:, 0:8], AF.Exp, scale=-1.0))
            p.wait("vector", "sS")
            v_step(lambda e: e.tensor_scalar(c_e, c_e, 1.0, None, op0=OP.add), inc=False)
            v_step(lambda e: e.reciprocal(c_s, c_e), inc=False)
            v_step(lambda e: e.tensor_tensor(c_s, c_s, c_gateb, op=OP.add), inc=False)
            sv = c_s.rearrange("p (g t) -> p g t", t=2)
            v_step(lambda e, sv=sv: e.tensor_tensor(c_G, sv[:, :, 0], sv[:, :, 1], op=OP.add), inc=False)
            gj = c_G.unsqueeze(1).broadcast_to([128, 4, 4])
            gi = c_G.unsqueeze(2).broadcast_to([128, 4, 4])
            v_step(lambda e, gj=gj, gi=gi: e.tensor_tensor(c_scr1[:, 0:16], gj, gi, op=OP.is_gt), inc=False)
            v_step(lambda e, gj=gj, gi=gi: e.tensor_tensor(c_scr2[:, 0:16], gj, gi, op=OP.is_equal), inc=False)
            v_step(lambda e: e.tensor_tensor(c_scr2[:, 0:16], c_scr2[:, 0:16], emask4, op=OP.mult), inc=False)
            v_step(lambda e: e.tensor_tensor(c_scr1[:, 0:16], c_scr1[:, 0:16], c_scr2[:, 0:16], op=OP.add), inc=False)
            v_step(lambda e: e.tensor_reduce(c_rank[:, 0:4], c_scr1[:, 0:16].rearrange("p (i j) -> p i j", j=4), axis=AX.X, op=OP.add), inc=False)
            v_step(lambda e: e.tensor_scalar(c_rank[:, 0:4], c_rank[:, 0:4], 1.5, None, op0=OP.is_lt), inc=False)
            gm = c_rank[:, 0:4].unsqueeze(2).broadcast_to([128, 4, 2])
            v_step(lambda e, gm=gm: e.tensor_tensor(c_m8.rearrange("p (g t) -> p g t", t=2), c_s.rearrange("p (g t) -> p g t", t=2), gm, op=OP.mult), inc=False)
            mj = c_m8.unsqueeze(1).broadcast_to([128, 8, 8])
            mi = c_m8.unsqueeze(2).broadcast_to([128, 8, 8])
            v_step(lambda e, mj=mj, mi=mi: e.tensor_tensor(c_scr1, mj, mi, op=OP.is_gt), inc=False)
            v_step(lambda e, mj=mj, mi=mi: e.tensor_tensor(c_scr2, mj, mi, op=OP.is_equal), inc=False)
            v_step(lambda e: e.tensor_tensor(c_scr2, c_scr2, emask8, op=OP.mult), inc=False)
            v_step(lambda e: e.tensor_tensor(c_scr1, c_scr1, c_scr2, op=OP.add), inc=False)
            v_step(lambda e: e.tensor_reduce(c_rank, c_scr1.rearrange("p (i j) -> p i j", j=8), axis=AX.X, op=OP.add), inc=False)
            v_step(lambda e: e.tensor_scalar(c_sel, c_rank, 1.5, None, op0=OP.is_lt), inc=False)
            v_step(lambda e: e.tensor_tensor(c_sw, c_sel, c_s, op=OP.mult), inc=False)
            v_step(lambda e: e.tensor_reduce(wsum, c_sw, axis=AX.X, op=OP.add), inc=False)
            v_step(lambda e: e.tensor_scalar(wsum, wsum, 1e-20, None, op0=OP.add), inc=False)
            v_step(lambda e: e.reciprocal(wnorm, wsum), inc=False)
            v_step(lambda e: e.tensor_scalar(wnorm, wnorm, ROUTED_SCALE, None, op0=OP.mult), inc=False)
            v_step(lambda e, j=j: e.tensor_scalar(WD[:, j * 8:(j + 1) * 8], c_sw, wnorm, None, op0=OP.mult))
        if t_wd is not None:
            p.wait("sync", "sV")
            dma("sync", t_wd[:, q * 32:(q + 1) * 32], WD[:], sem="sO")
            p.wait("vector", "sO")


    prep_quarter(0)
    for q in range(NQ):
        tiles = [q * NT_Q + j for j in range(NT_Q)]
        for e_i in range(NE):
            routed = e_i < 8
            si_g = e_i * 3
            si_u = e_i * 3 + 1
            si_d = e_i * 3 + 2
            gwh_, gwl_ = ring_views(slot_of[(q, si_g)], "g")
            uwh_, uwl_ = ring_views(slot_of[(q, si_u)], "u")
            dwh_, dwl_ = ring_views(slot_of[(q, si_d)], "d")
            p.wait("tensor", f"sW{slot_of[(q, si_g)]}", load_val[(q, si_g)])
            for j in range(NT_Q):
                gb = 2 + j % 2
                p.wait("tensor", "sS", s_markPS[gb])
                p.wait("tensor", "sV", v_markPS[gb])
                for kc in range(8):
                    mm3(PS[gb],
                        c_xTqh[:, j * 1024 + kc * 128:j * 1024 + (kc + 1) * 128],
                        c_xTql[:, j * 1024 + kc * 128:j * 1024 + (kc + 1) * 128],
                        gwh_(kc), gwl_(kc),
                        start=(kc == 0), stop=(kc == 7), inc=(kc == 7))
                p.wait("scalar", "sT")
                p.wait("scalar", "sV", gu_mark[j % 2])
                s_markPS[gb] = s_step(lambda e, j=j, gb=gb: e.activation(c_g[j], PS[gb], AF.Silu))
            free_slot(q, si_g)
            p.wait("tensor", f"sW{slot_of[(q, si_u)]}", load_val[(q, si_u)])
            for j in range(NT_Q):
                gb = 2 + j % 2
                p.wait("tensor", "sV", v_markPS[gb])
                p.wait("tensor", "sS", s_markPS[gb])
                for kc in range(8):
                    mm3(PS[gb],
                        c_xTqh[:, j * 1024 + kc * 128:j * 1024 + (kc + 1) * 128],
                        c_xTql[:, j * 1024 + kc * 128:j * 1024 + (kc + 1) * 128],
                        uwh_(kc), uwl_(kc),
                        start=(kc == 0), stop=(kc == 7), inc=(kc == 7))
                p.wait("vector", "sT")
                p.wait("vector", "sS")
                if routed:
                    m_ = v_step(lambda e, j=j, e_i=e_i, gb=gb: e.scalar_tensor_tensor(
                        c_gu[j], PS[gb], WD[:, j * 8 + e_i:j * 8 + e_i + 1],
                        c_g[j], op0=OP.mult, op1=OP.mult))
                else:
                    m_ = v_step(lambda e, j=j, gb=gb: e.tensor_tensor(c_gu[j], PS[gb], c_g[j], op=OP.mult))
                v_markPS[gb] = m_
                gu_mark[j % 2] = m_
            free_slot(q, si_u)
            p.wait("tensor", f"sW{slot_of[(q, si_d)]}", load_val[(q, si_d)])
            for j in range(NT_Q):
                db = j % 2
                p.wait("tensor", "sV", max(marks["ps01"], gu_mark[j % 2]))
                for kc in range(4):
                    t_step(lambda e, j=j, kc=kc, db=db: e.transpose(
                        PS[db][:, kc * 128:(kc + 1) * 128],
                        c_gu[j][:, kc * 128:(kc + 1) * 128], ident),
                        inc=(kc == 3))
                p.wait("vector", "sT")
                guts = split_v(c_guTh[db][:, :], c_guTl[db][:, :], PS[db])
                marks["ps01"] = guts
                p.wait("tensor", "sV", max(guts, yq_mark[db]))
                for oc in range(2):
                    for kc in range(4):
                        mm3(PDD[db][:, oc * 512:(oc + 1) * 512],
                            c_guTh[db][:, kc * 128:(kc + 1) * 128],
                            c_guTl[db][:, kc * 128:(kc + 1) * 128],
                            dwh_(kc, oc), dwl_(kc, oc),
                            start=(kc == 0), stop=(kc == 3),
                            inc=(oc == 1 and kc == 3))
                p.wait("vector", "sT")
                if e_i == 0:
                    yq_mark[db] = v_step(lambda e, j=j, db=db: e.tensor_copy(c_Yq[j], PDD[db][:, 0:1024]))
                else:
                    yq_mark[db] = v_step(lambda e, j=j, db=db: e.tensor_tensor(c_Yq[j], c_Yq[j], PDD[db][:, 0:1024], op=OP.add))
            free_slot(q, si_d)


        if q == 0 and t_y0 is not None:
            p.wait("sync", "sV")
            dma("sync", t_y0[:], c_Yq[0], sem="sO")
            p.wait("vector", "sO")

        if q + 1 < NQ:
            prep_quarter(q + 1)

        for j, ti_ in enumerate(tiles):
            ob = c_out[j % 2]
            if q * NT_Q + j >= 2:
                p.wait("vector", f"sY{j % 2}", out_dma.get(q * NT_Q + j - 2, 0))
            v_step(lambda e, j=j, ti_=ti_, ob=ob: e.tensor_tensor(ob[:], c_Yq[j], X2[ti_], op=OP.add))
            p.wait("scalar", "sV")
            s_step(lambda e, ob=ob: e.activation(c_scrA, ob[:], AF.Copy, accum_out=ssum), inc=False)
            s_step(lambda e, ob=ob: e.activation(c_scrA, ob[:], AF.Square, accum_out=ssq))
            p.wait("vector", "sS")
            v_step(lambda e: e.tensor_scalar(smean, ssum, 1.0 / DIM, None, op0=OP.mult), inc=False)
            v_step(lambda e: e.tensor_tensor(st1, smean, smean, op=OP.mult), inc=False)
            v_step(lambda e: e.scalar_tensor_tensor(svar, ssq, 1.0 / DIM, st1,
                                                    op0=OP.mult, op1=OP.subtract))
            p.wait("scalar", "sV")
            s_step(lambda e: e.activation(st1, svar, AF.Ln, bias=c_lneps), inc=False)
            s_step(lambda e: e.activation(rstd, st1, AF.Exp, scale=-0.5))
            p.wait("vector", "sS")
            v_step(lambda e, ob=ob: e.tensor_scalar(ob[:], ob[:], smean, rstd,
                                                    op0=OP.subtract, op1=OP.mult))
            p.wait("sync", "sV")
            out_dma[q * NT_Q + j] = dma("sync", y[ti_ * 128:(ti_ + 1) * 128, :],
                                        ob[:], sem=f"sY{j % 2}")

    for s in ("sY0", "sY1", "sO"):
        p.on("sync", lambda e, s=s: e.wait_ge(p.sems[s], p.cnt[s]) if p.cnt[s] else None)

    p.replay()
    return nc, tap_t


def kernel(**inputs):
    x = np.asarray(inputs["L0_attn_x"])
    Wqkv = np.asarray(inputs["L0_attn_Wqkv"])
    bqkv = np.asarray(inputs["L0_attn_bqkv"])
    gq = np.asarray(inputs["L0_attn_gq"])
    gk = np.asarray(inputs["L0_attn_gk"])
    Wout = np.asarray(inputs["L0_attn_Wout"])
    bout = np.asarray(inputs["L0_attn_bout"])
    gout = np.asarray(inputs["L0_attn_gout"])
    gate_w = np.asarray(inputs["L0_ffn_gate_w"])
    gate_b = np.asarray(inputs["L0_ffn_gate_bias"])
    w_gate = np.asarray(inputs["L0_ffn_w_gate"])
    w_up = np.asarray(inputs["L0_ffn_w_up"])
    w_down = np.asarray(inputs["L0_ffn_w_down"])
    sw_gate = np.asarray(inputs["L0_ffn_sw_gate"])
    sw_up = np.asarray(inputs["L0_ffn_sw_up"])
    sw_down = np.asarray(inputs["L0_ffn_sw_down"])

    nc, _ = build_program()
    C = np.ascontiguousarray
    shared = {
        "wkvT": C(Wqkv[DIM:3 * DIM].T),
        "wqT": C(Wqkv[0:DIM].T),
        "woutT": C(Wout.T),
        "bqkv": C(bqkv.reshape(1, 3 * DIM)),
        "gqv": C(gq.reshape(1, DIM)),
        "gkv": C(gk.reshape(1, DIM)),
        "goutv": C(gout.reshape(1, DIM)),
        "boutv": C(bout.reshape(1, DIM)),
        "gatew": C(gate_w),
        "gateb": C(gate_b.reshape(1, 8)),
        "wg": C(w_gate),
        "wu": C(w_up),
        "wd": C(w_down),
        "swg": C(sw_gate),
        "swu": C(sw_up),
        "swd": C(sw_down),
    }
    in_maps = []
    for c in range(8):
        b, h = divmod(c, 2)
        own = x[b, h * 2048:(h + 1) * 2048]
        fore = x[b, (1 - h) * 2048:(2 - h) * 2048]
        xbc = np.concatenate([own, fore], axis=0)
        poff = np.zeros((1, NT_ALL), dtype=np.float32)
        poff[0, :16] = h * 2048
        poff[0, 16:] = (1 - h) * 2048 - 2048
        m = dict(shared)
        m["xb"] = C(xbc)
        m["posoff"] = poff
        in_maps.append(m)

    res = run_bass_kernel_spmd(nc, in_maps, core_ids=list(range(8)))
    out = np.empty((4, 4096, DIM), dtype=np.float32)
    for c in range(8):
        b, h = divmod(c, 2)
        out[b, h * 2048:(h + 1) * 2048] = res.results[c]["y"]
    return out


# version 18 (final)
# speedup vs baseline: 1.0827x; 1.0696x over previous
import math
import numpy as np
import concourse.bass as bass
import concourse.mybir as mybir
from concourse.alu_op_type import AluOpType as OP
from concourse.bass_utils import run_bass_kernel_spmd

F32 = mybir.dt.float32
I32 = mybir.dt.int32
BF16 = mybir.dt.bfloat16
AF = mybir.ActivationFunctionType
AX = mybir.AxisListType

DIM = 1024
NH = 16
RMS_EPS = 1.1920929e-07
EPS_LIN = 1e-06
LN_EPS = 1e-05
THETA = 10000.0
ROUTED_SCALE = 2.5
NT_ALL = 32
NT_OWN = 16
NQ = 4
NT_Q = NT_OWN // NQ
NE = 10
NSLOT = 3 * NE

TWO_PI = 2.0 * math.pi
CW1 = 6.28125
CW2 = float(np.float32(TWO_PI - CW1))
CW3 = float(TWO_PI - CW1 - CW2)

ENGINES = ("sync", "gpsimd", "vector", "scalar", "tensor")


class Prog:

    def __init__(self, nc):
        self.nc = nc
        self.steps = {e: [] for e in ENGINES}
        self.sems = {}
        self.cnt = {}

    def sem(self, name):
        h = self.nc.alloc_semaphore(name)
        self.sems[name] = h
        self.cnt[name] = 0
        return h

    def on(self, eng, fn):
        self.steps[eng].append(fn)

    def wait(self, eng, sem_name, val=None):
        if val is None:
            val = self.cnt[sem_name]
        h = self.sems[sem_name]
        if val > 0:
            self.on(eng, lambda e, h=h, v=val: e.wait_ge(h, v))

    def inc(self, sem_name, amount=1):
        self.cnt[sem_name] += amount
        return self.cnt[sem_name]

    def tag(self, ins_fn, sem_name, amount=1):
        import traceback
        h = self.sems[sem_name]
        self.inc(sem_name, amount)
        site = "".join(traceback.format_stack(limit=6)[:-1])

        def f(e, ins_fn=ins_fn, h=h, amount=amount, site=site):
            ins = ins_fn(e)
            if ins is None:
                raise RuntimeError("op returned None; emitted at:\n" + site)
            ins.then_inc(h, amount)
        return f

    def replay(self):
        nc = self.nc
        with nc.Block() as block:
            @block.sync
            def _(e):
                for f in self.steps["sync"]:
                    f(e)

            @block.gpsimd
            def _(e):
                for f in self.steps["gpsimd"]:
                    f(e)

            @block.vector
            def _(e):
                for f in self.steps["vector"]:
                    f(e)

            @block.scalar
            def _(e):
                for f in self.steps["scalar"]:
                    f(e)

            @block.tensor
            def _(e):
                for f in self.steps["tensor"]:
                    f(e)


def build_program(taps=(), phases=3, race_safe=False):
    taps = set(taps)
    nc = bass.Bass("TRN2", target_bir_lowering=False)

    xb = nc.dram_tensor("xb", [NT_ALL * 128, DIM], F32, kind="ExternalInput").ap()
    wkvT = nc.dram_tensor("wkvT", [DIM, 2 * DIM], F32, kind="ExternalInput").ap()
    wqT = nc.dram_tensor("wqT", [DIM, DIM], F32, kind="ExternalInput").ap()
    woutT = nc.dram_tensor("woutT", [DIM, DIM], F32, kind="ExternalInput").ap()
    bqkv = nc.dram_tensor("bqkv", [1, 3 * DIM], F32, kind="ExternalInput").ap()
    gqv = nc.dram_tensor("gqv", [1, DIM], F32, kind="ExternalInput").ap()
    gkv = nc.dram_tensor("gkv", [1, DIM], F32, kind="ExternalInput").ap()
    goutv = nc.dram_tensor("goutv", [1, DIM], F32, kind="ExternalInput").ap()
    boutv = nc.dram_tensor("boutv", [1, DIM], F32, kind="ExternalInput").ap()
    gatew = nc.dram_tensor("gatew", [DIM, 8], F32, kind="ExternalInput").ap()
    gateb = nc.dram_tensor("gateb", [1, 8], F32, kind="ExternalInput").ap()
    wg = nc.dram_tensor("wg", [8, DIM, 512], F32, kind="ExternalInput").ap()
    wu = nc.dram_tensor("wu", [8, DIM, 512], F32, kind="ExternalInput").ap()
    wd = nc.dram_tensor("wd", [8, 512, DIM], F32, kind="ExternalInput").ap()
    swg = nc.dram_tensor("swg", [DIM, DIM], F32, kind="ExternalInput").ap()
    swu = nc.dram_tensor("swu", [DIM, DIM], F32, kind="ExternalInput").ap()
    swd = nc.dram_tensor("swd", [DIM, DIM], F32, kind="ExternalInput").ap()
    posoff = nc.dram_tensor("posoff", [1, NT_ALL], F32, kind="ExternalInput").ap()
    y = nc.dram_tensor("y", [NT_OWN * 128, DIM], F32, kind="ExternalOutput").ap()
    wsp = nc.dram_tensor("wsp", [NSLOT, 128, 8192], BF16, kind="Internal").ap()
    wspq = nc.dram_tensor("wspq", [4, 128, 8192], BF16, kind="Internal").ap()

    tap_t = {}
    def tap(name, shape):
        if name in taps:
            tap_t[name] = nc.dram_tensor("tap_" + name, shape, F32,
                                         kind="ExternalOutput").ap()
            return tap_t[name]
        return None

    t_vkt = tap("vkt", [128, NH * 65])
    t_kf0 = tap("kf0", [128, DIM])
    t_x2 = tap("x2", [NT_OWN * 128, DIM])
    t_wd = tap("wdense", [128, NQ * NT_Q * 8])
    t_qf0 = tap("qf0", [128, DIM])
    t_attn0 = tap("attn0", [128, DIM])
    t_y0 = tap("ffny0", [128, DIM])

    NAR = 53200
    AR = nc.alloc_sbuf_tensor("AR", [128, NAR], F32)
    ARP = AR.ap()

    def V(off, n):
        assert off >= 0 and off + n <= NAR, (off, n)
        return ARP[:, off:off + n]

    def VB(off, n_units):
        return V(off, n_units).bitcast(BF16)

    ident = V(0, 128)
    c_rmseps = V(128, 1)
    c_lneps = V(129, 1)
    ssq = V(130, 1)
    rstd = V(131, 1)
    smean = V(132, 1)
    svar = V(133, 1)
    st1 = V(134, 1)
    ssum = V(135, 1)
    WD = V(136, 32)
    wsum = V(168, 1)
    wnorm = V(169, 1)
    dnm = V(176, 16)
    rcp = V(192, 16)
    emask4 = V(216, 16)
    emask8 = V(232, 64)
    poffv = V(296, 32)

    X2_BASE = 512
    X2 = [V(X2_BASE + i * DIM, DIM) for i in range(NT_OWN)]

    MID = X2_BASE + NT_OWN * DIM
    cos_tab = V(MID, 1024)
    sin_tab = V(MID + 1024, 1024)
    VKTh = VB(MID + 2048, 520)
    VKTl = VB(MID + 2568, 520)
    posf = V(MID + 3088, 32)
    invf = V(MID + 3120, 32)
    B = MID + 3152
    assert NAR - B >= 33000

    a1_WKVh = VB(B, 8192)
    a1_WKVl = VB(B + 8192, 8192)
    a1_bkv = V(B + 16384, 2048)
    a1_gk = V(B + 18432, 1024)
    a1_xin = [V(B + 19456, 1024), V(B + 20480, 1024)]
    a1_xTh = VB(B + 21504, 512)
    a1_xTl = VB(B + 22016, 512)
    a1_kv = V(B + 22528, 2048)
    a1_kn = V(B + 24576, 1024)
    a1_kfh = [VB(B + 25600, 512), VB(B + 26624, 512)]
    a1_kfl = [VB(B + 26112, 512), VB(B + 27136, 512)]
    a1_vph = [VB(B + 27648, 520), VB(B + 28688, 520)]
    a1_vpl = [VB(B + 28168, 520), VB(B + 29208, 520)]
    a1_scr = V(B + 29728, 1024)
    assert B + 30752 <= NAR
    spw_a = V(B + 30752, 1024)
    spw_b = V(B + 31776, 1024)
    spw_hi = VB(B + 31776, 512)
    spw_lo = VB(B + 32288, 512)
    assert B + 32800 <= NAR

    a2_WQh = VB(B, 4096)
    a2_WQl = VB(B + 4096, 4096)
    a2_WOh = VB(B + 8192, 4096)
    a2_WOl = VB(B + 12288, 4096)
    a2_bq = V(B + 16384, 1024)
    a2_gq = V(B + 17408, 1024)
    a2_bout = V(B + 18432, 1024)
    a2_gout = V(B + 19456, 1024)
    a2_xin = [V(B + 20480, 1024), V(B + 21504, 1024)]
    a2_TSh = VB(B + 22528, 512)
    a2_TSl = VB(B + 23040, 512)
    a2_b1 = V(B + 23552, 1024)
    a2_b2 = V(B + 24576, 1024)
    a2_b3 = V(B + 25600, 1024)
    a2_b4 = V(B + 26624, 1024)
    a2_scr = V(B + 27648, 1024)
    sp_in = [V(B + 28672, 1024), V(B + 29696, 1024)]
    sp_out = [V(B + 30720, 1024), V(B + 31744, 1024)]
    assert B + 32768 <= NAR

    c_ring = [VB(B + s * 4096, 4096) for s in range(4)]
    c_xTqh = VB(B + 16384, 2048)
    c_xTql = VB(B + 18432, 2048)
    c_Yq = [V(B + 20480 + j * 1024, 1024) for j in range(NT_Q)]
    c_g = [V(B + 24576 + j * 512, 512) for j in range(NT_Q)]
    c_gu = [V(B + 26624 + j * 512, 512) for j in range(NT_Q)]
    c_guTh = [VB(B + 28672, 256), VB(B + 32584, 256)]
    c_guTl = [VB(B + 28928, 256), VB(B + 32840, 256)]
    c_gwf = V(B + 29184, 64)
    c_gwh = VB(B + 29248, 32)
    c_gwl = VB(B + 29280, 32)
    c_gateb = V(B + 29312, 8)
    c_out = [V(B + 29328, 1024), V(B + 30352, 1024)]
    c_s = V(B + 31376, 8)
    c_e = V(B + 31384, 8)
    c_G = V(B + 31392, 4)
    c_m8 = V(B + 31400, 8)
    c_rank = V(B + 31408, 8)
    c_sel = V(B + 31416, 8)
    c_sw = V(B + 31424, 8)
    c_scr1 = V(B + 31432, 64)
    c_scr2 = V(B + 31496, 64)
    c_scrA = V(B + 31560, 1024)
    assert B + 33096 <= NAR

    s_identi = V(B + 26624, 128).bitcast(I32)
    s_posi = V(B + 27648, 32).bitcast(I32)
    s_ki = V(B + 25600, 1024).bitcast(I32)
    s_theta = a1_kn
    s_fold = a1_scr
    s_kf = a1_kv[:, 0:1024]
    s_emi4 = a1_kv[:, 1024:1040].bitcast(I32)
    s_emi8 = a1_kv[:, 1040:1104].bitcast(I32)

    PS = [nc.alloc_psum_tensor(f"ps{i}", [128, 512], F32).ap() for i in range(4)]
    PDD = [nc.alloc_psum_tensor(f"pd{i}", [128, 1024], F32).ap() for i in range(2)]

    def vk_bank(h, rows=128):
        g, k = divmod(h, 4)
        return PDD[g // 2][0:rows, (g % 2) * 512 + k * 65:(g % 2) * 512 + k * 65 + 65]

    p = Prog(nc)
    for s in ("sP", "sPG", "sX0", "sX1", "sWA", "sW0", "sW1", "sW2", "sW3",
              "sQ0", "sQ1", "sQ2", "sQ3",
              "sVD", "sY0", "sY1", "sO", "sWS0", "sWS1", "sWB0", "sWB1",
              "sT", "sV", "sS", "sG"):
        p.sem(s)

    def dma(eng, out, in_, sem="sP"):
        h = p.sems[sem]
        p.inc(sem, 16)
        p.on(eng, lambda e, out=out, in_=in_, h=h:
             e.dma_start(out=out, in_=in_).then_inc(h, 16))
        return p.cnt[sem]

    def t_step(ins_fn, inc=True):
        if inc:
            p.on("tensor", p.tag(ins_fn, "sT"))
        else:
            p.on("tensor", ins_fn)
        return p.cnt["sT"]

    def v_step(ins_fn, inc=True):
        p.on("vector", lambda e: e.drain())
        if inc:
            p.on("vector", p.tag(ins_fn, "sV"))
        else:
            p.on("vector", ins_fn)
        return p.cnt["sV"]

    def s_step(ins_fn, inc=True):
        p.on("scalar", lambda e: e.drain())
        if inc:
            p.on("scalar", p.tag(ins_fn, "sS"))
        else:
            p.on("scalar", ins_fn)
        return p.cnt["sS"]

    def g_step(ins_fn, inc=True):
        p.on("gpsimd", lambda e: e.drain())
        if inc:
            p.on("gpsimd", p.tag(ins_fn, "sG"))
        else:
            p.on("gpsimd", ins_fn)
        return p.cnt["sG"]

    def split_v(hi, lo, src, inc=True):
        v_step(lambda e: e.tensor_copy(hi, src), inc=False)
        return v_step(lambda e: e.tensor_tensor(lo, src, hi, op=OP.subtract), inc=inc)

    def split_g(hi, lo, src, inc=True):
        g_step(lambda e: e.tensor_copy(hi, src), inc=False)
        return g_step(lambda e: e.tensor_tensor(lo, src, hi, op=OP.subtract), inc=inc)

    marks = {"ps01": 0}

    def transpose8s(src, dsth, dstl, n=8, src_mark=0):
        assert n % 2 == 0
        p.wait("tensor", "sV", max(marks["ps01"], src_mark))
        for kc in range(n):
            bank = PS[kc % 2]
            off = (kc // 2) * 128
            t_step(lambda e, bank=bank, off=off, kc=kc, src=src:
                   e.transpose(bank[:, off:off + 128],
                               src[:, kc * 128:(kc + 1) * 128], ident),
                   inc=(kc == n - 1))
        p.wait("vector", "sT")
        nb = (n // 2) * 128
        for b in range(2):
            hv = dsth.rearrange("p (a two c) -> p a two c", two=2, c=128)[:, :, b, :]
            lv = dstl.rearrange("p (a two c) -> p a two c", two=2, c=128)[:, :, b, :]
            sv = PS[b][:, 0:nb].rearrange("p (a c) -> p a c", c=128)
            split_v(hv, lv, sv, inc=(b == 1))
        marks["ps01"] = p.cnt["sV"]
        return marks["ps01"]

    def mm3(bank, sh, sl, mh, ml, start, stop, inc=False):
        t_step(lambda e: e.matmul(bank, sh, mh, start=start, stop=False,
                                  skip_group_check=True), inc=False)
        t_step(lambda e: e.matmul(bank, sh, ml, start=False, stop=False,
                                  skip_group_check=True), inc=False)
        return t_step(lambda e: e.matmul(bank, sl, mh, start=False, stop=stop,
                                         skip_group_check=True), inc=inc)

    def w_chunk(kind, e, cc):
        if kind == "g":
            t = wg[e] if e < 8 else swg[:, (e - 8) * 512:(e - 7) * 512]
        elif kind == "u":
            t = wu[e] if e < 8 else swu[:, (e - 8) * 512:(e - 7) * 512]
        else:
            t = wd[e] if e < 8 else swd[(e - 8) * 512:(e - 7) * 512, :]
        if kind in ("g", "u"):
            return t.rearrange("(a p) n -> p a n", p=128)[:, 2 * cc:2 * cc + 2, :]
        return t.rearrange("(a p) n -> p a n", p=128)[:, cc:cc + 1, :]

    slot_key = [(e0, k) for e0 in range(NE) for k in ("g", "u", "d")]
    sp_jobs = [(si, cc) for si in range(NSLOT) for cc in range(4)]
    wsp_wbs = [dict() for _ in range(NSLOT)]

    dx_bkv = dma("sync", a1_bkv[:], bqkv[0:1, DIM:3 * DIM].broadcast_to([128, 2048]))
    dx_gk = dma("sync", a1_gk[:], gkv[0:1, :].broadcast_to([128, DIM]))
    dx_xin = {}
    dx_xin[0] = dma("sync", a1_xin[0][:], xb[0:128, :], sem="sX0")
    dx_poff = dma("sync", poffv, posoff[0:1, :].broadcast_to([128, NT_ALL]))

    g_step(lambda e: e.iota(s_identi, pattern=[[-1, 128]], base=0, channel_multiplier=1))
    p.wait("vector", "sG")
    v_step(lambda e: e.tensor_scalar(ident, s_identi, 0, None, op0=OP.is_equal))
    g_step(lambda e: e.iota(s_emi4, pattern=[[1, 4], [-1, 4]], base=0, channel_multiplier=0))
    g_step(lambda e: e.iota(s_emi8, pattern=[[1, 8], [-1, 8]], base=0, channel_multiplier=0))
    p.wait("vector", "sG")
    v_step(lambda e: e.tensor_scalar(emask4, s_emi4, 0, None, op0=OP.is_gt), inc=False)
    v_step(lambda e: e.tensor_scalar(emask8, s_emi8, 0, None, op0=OP.is_gt), inc=False)
    v_step(lambda e: e.memset(VKTh[:, :], 0.0), inc=False)
    v_step(lambda e: e.memset(VKTl[:, :], 0.0), inc=False)
    v_step(lambda e: e.memset(c_rmseps, RMS_EPS), inc=False)
    v_step(lambda e: e.memset(c_lneps, LN_EPS), inc=False)
    for bi in range(2):
        v_step(lambda e, bi=bi: e.memset(
            a1_vph[bi].rearrange("p (h d) -> p h d", d=65)[:, :, 64:65], 1.0),
            inc=False)
        v_step(lambda e, bi=bi: e.memset(a1_vpl[bi][:, :], 0.0), inc=False)

    g_step(lambda e: e.iota(s_posi, pattern=[[128, NT_ALL]], base=0, channel_multiplier=1))
    p.wait("vector", "sG")
    v_step(lambda e: e.tensor_copy(posf, s_posi), inc=False)
    p.wait("vector", "sP")
    v_step(lambda e: e.tensor_tensor(posf, posf, poffv, op=OP.add))
    for i_ in range(32):
        val = float(np.float32(1.0 / THETA ** (i_ / 32.0)))
        v_step(lambda e, i_=i_, val=val: e.memset(invf[:, i_:i_ + 1], val),
               inc=(i_ == 31))
    v_step(lambda e: e.tensor_tensor(
        s_theta[:].rearrange("p (t i) -> p t i", i=32),
        posf.unsqueeze(2).broadcast_to([128, 32, 32]),
        invf.unsqueeze(1).broadcast_to([128, 32, 32]), op=OP.mult))

    def sincos(dst, shift):
        if shift:
            v_step(lambda e: e.tensor_scalar(s_fold, s_theta, shift, None, op0=OP.add), inc=False)
            src = s_fold
        else:
            src = s_theta
        v_step(lambda e, src=src: e.tensor_scalar(dst, src, 1.0 / TWO_PI, None,
                                                  op0=OP.mult), inc=False)
        v_step(lambda e: e.tensor_copy(s_ki, dst), inc=False)
        v_step(lambda e: e.tensor_copy(s_kf, s_ki), inc=False)
        for ci, c in enumerate((CW1, CW2, CW3)):
            v_step(lambda e, c=c, src=src: e.scalar_tensor_tensor(
                dst, s_kf, -c, src if c == CW1 else dst,
                op0=OP.mult, op1=OP.add), inc=False)
        v_step(lambda e: e.tensor_scalar(s_kf, dst, math.pi, None, op0=OP.is_gt), inc=False)
        v_step(lambda e: e.scalar_tensor_tensor(dst, s_kf, -TWO_PI, dst,
                                                op0=OP.mult, op1=OP.add), inc=False)
        v_step(lambda e: e.tensor_scalar(s_kf, dst, -math.pi, None, op0=OP.is_lt), inc=False)
        v_step(lambda e: e.scalar_tensor_tensor(dst, s_kf, TWO_PI, dst,
                                                op0=OP.mult, op1=OP.add), inc=False)
        v_step(lambda e: e.tensor_scalar(dst, dst, math.pi, None, op0=OP.min), inc=False)
        v_step(lambda e: e.tensor_scalar(dst, dst, -math.pi, None, op0=OP.max))
        p.wait("scalar", "sV")
        s_step(lambda e: e.activation(dst, dst, AF.Sin))

    sincos(sin_tab, 0.0)
    sincos(cos_tab, math.pi / 2.0)
    p.wait("vector", "sS")
    marks["ps01"] = p.cnt["sV"]

    wkv_v = wkvT[:].rearrange("(a p) (b n) -> p a b n", p=128, n=1024)
    dwk = {}
    for c in range(16):
        stg = spw_a if c % 2 == 0 else spw_b
        dwk[c] = dma("gpsimd", stg, wkv_v[:, c // 2, c % 2, :],
                     sem=f"sWS{c % 2}")
        p.wait("gpsimd", f"sWS{c % 2}", dwk[c])
        split_g(a1_WKVh[:, c * 1024:(c + 1) * 1024],
                a1_WKVl[:, c * 1024:(c + 1) * 1024], stg)
    wkv_mark = p.cnt["sG"]

    wq_v = wqT[:].rearrange("(a p) n -> p a n", p=128)
    wo_v = woutT[:].rearrange("(a p) n -> p a n", p=128)
    wq_out_done = 0
    for c in range(16):
        src = wq_v[:, c, :] if c < 8 else wo_v[:, c - 8, :]
        dk = dma("gpsimd", spw_a, src, sem="sWS0")
        p.wait("gpsimd", "sWS0", dk)
        if wq_out_done:
            p.wait("gpsimd", "sWA", wq_out_done)
        gmark = split_g(spw_hi, spw_lo, spw_a)
        p.wait("gpsimd", "sG", gmark)
        dma("gpsimd", wspq[2 * (c // 8), :, (c % 8) * 1024:(c % 8 + 1) * 1024],
            spw_hi, sem="sWA")
        wq_out_done = dma(
            "gpsimd", wspq[2 * (c // 8) + 1, :, (c % 8) * 1024:(c % 8 + 1) * 1024],
            spw_lo, sem="sWA")

    SP_START = 120
    spw_pair = V(B + 31776, 1024).bitcast(BF16)
    a1_wb_last = {}
    for ji in range(SP_START):
        si, cc = sp_jobs[ji]
        e0, kind = slot_key[si]
        nsub = 512 if kind in ("g", "u") else 1024
        dk = dma("gpsimd", spw_a.rearrange("p (a n) -> p a n", n=nsub),
                 w_chunk(kind, e0, cc), sem="sWS0")
        p.wait("gpsimd", "sWS0", dk)
        if ji == 0:
            p.wait("gpsimd", "sWA", wq_out_done)
        else:
            par_p = (ji - 1) % 2
            p.wait("gpsimd", f"sWB{par_p}", a1_wb_last[par_p])
        gmark = split_g(spw_hi, spw_lo, spw_a)
        p.wait("gpsimd", "sG", gmark)
        a1_wb_last[ji % 2] = dma("gpsimd", wsp[si, :, cc * 2048:(cc + 1) * 2048],
                                 spw_pair, sem=f"sWB{ji % 2}")
        wsp_wbs[si][f"sWB{ji % 2}"] = a1_wb_last[ji % 2]

    mark_tr = {}
    mark_vk = {}
    bank_mark = {2: 0, 3: 0}
    kfvp_mark = {0: 0, 1: 0}

    p.wait("tensor", "sX0", dx_xin[0])
    ts_marks = {0: transpose8s(a1_xin[0], a1_xTh, a1_xTl)}
    mark_tr[0] = p.cnt["sT"]
    for t in range(NT_ALL):
        if t + 1 < NT_ALL:
            if t - 1 >= 0:
                p.wait("sync", "sT", mark_tr[t - 1])
            dx_xin[t + 1] = dma("sync", a1_xin[(t + 1) % 2][:],
                                xb[(t + 1) * 128:(t + 2) * 128, :],
                                sem=f"sX{(t + 1) % 2}")
        ts_mark = ts_marks[t]
        if t == 0:
            p.wait("tensor", "sG", wkv_mark)
        for oc in range(4):
            bank = PS[2 + oc % 2]
            p.wait("tensor", "sV", max(bank_mark[2 + oc % 2], ts_mark))
            for kc in range(8):
                mm3(bank,
                    a1_xTh[:, kc * 128:(kc + 1) * 128],
                    a1_xTl[:, kc * 128:(kc + 1) * 128],
                    a1_WKVh[:, kc * 2048 + oc * 512:kc * 2048 + (oc + 1) * 512],
                    a1_WKVl[:, kc * 2048 + oc * 512:kc * 2048 + (oc + 1) * 512],
                    start=(kc == 0), stop=(kc == 7), inc=(kc == 7))
            p.wait("vector", "sT")
            if t == 0 and oc == 0:
                p.wait("vector", "sP")
            bank_mark[2 + oc % 2] = v_step(lambda e, bank=bank, oc=oc: e.tensor_tensor(
                a1_kv[:, oc * 512:(oc + 1) * 512], bank,
                a1_bkv[:, oc * 512:(oc + 1) * 512], op=OP.add))
        kv_evac_mark = p.cnt["sV"]
        if t + 1 < NT_ALL:
            p.wait("tensor", f"sX{(t + 1) % 2}", dx_xin[t + 1])
            ts_marks[t + 1] = transpose8s(a1_xin[(t + 1) % 2], a1_xTh, a1_xTl)
            mark_tr[t + 1] = p.cnt["sT"]
        if t >= 1:
            tt = t - 1
            p.wait("tensor", "sV", kfvp_mark[tt % 2])
            for h in range(NH):
                mm3(vk_bank(h, 64),
                    a1_kfh[tt % 2][:, h * 64:(h + 1) * 64],
                    a1_kfl[tt % 2][:, h * 64:(h + 1) * 64],
                    a1_vph[tt % 2][:, h * 65:(h + 1) * 65],
                    a1_vpl[tt % 2][:, h * 65:(h + 1) * 65],
                    start=(tt == 0 and h % 4 == 0), stop=False,
                    inc=(h == NH - 1))
            mark_vk[tt] = p.cnt["sT"]
        p.wait("scalar", "sV", kv_evac_mark)
        s_step(lambda e: e.activation(a1_scr, a1_kv[:, 0:1024], AF.Square,
                                      accum_out=ssq), inc=False)
        s_step(lambda e: e.activation(st1, ssq, AF.Ln, scale=1.0 / DIM,
                                      bias=c_rmseps), inc=False)
        s_step(lambda e: e.activation(rstd, st1, AF.Exp, scale=-0.5))
        p.wait("vector", "sS")
        if t == 0:
            p.wait("vector", "sP")
        v_step(lambda e: e.scalar_tensor_tensor(a1_kn, a1_kv[:, 0:1024], rstd,
                                                a1_gk, op0=OP.mult, op1=OP.mult), inc=False)
        cs = cos_tab[:, t * 32:(t + 1) * 32].unsqueeze(1).broadcast_to([128, 16, 32])
        sn = sin_tab[:, t * 32:(t + 1) * 32].unsqueeze(1).broadcast_to([128, 16, 32])
        knv = a1_kn[:].rearrange("p (h d two) -> p h d two", h=16, two=2)
        kro = a1_kv[:, 0:1024]
        krv = kro.rearrange("p (h d two) -> p h d two", h=16, two=2)
        tr, ti = knv[:, :, :, 0], knv[:, :, :, 1]
        sc1 = a1_scr[:, 0:512].rearrange("p (h d) -> p h d", d=32)
        sc2 = a1_scr[:, 512:1024].rearrange("p (h d) -> p h d", d=32)
        v_step(lambda e, tr=tr, cs=cs, sc1=sc1: e.tensor_tensor(sc1, tr, cs, op=OP.mult), inc=False)
        v_step(lambda e, ti=ti, sn=sn, sc2=sc2: e.tensor_tensor(sc2, ti, sn, op=OP.mult), inc=False)
        v_step(lambda e, krv=krv, sc1=sc1, sc2=sc2: e.tensor_tensor(krv[:, :, :, 0], sc1, sc2, op=OP.subtract), inc=False)
        v_step(lambda e, tr=tr, sn=sn, sc1=sc1: e.tensor_tensor(sc1, tr, sn, op=OP.mult), inc=False)
        v_step(lambda e, ti=ti, cs=cs, sc2=sc2: e.tensor_tensor(sc2, ti, cs, op=OP.mult), inc=False)
        v_step(lambda e, krv=krv, sc1=sc1, sc2=sc2: e.tensor_tensor(krv[:, :, :, 1], sc1, sc2, op=OP.add), inc=False)
        v_step(lambda e, kro=kro: e.tensor_scalar(kro, kro, 0.0, None, op0=OP.max), inc=False)
        split_v(a1_kfh[t % 2][:, :], a1_kfl[t % 2][:, :], kro, inc=False)
        vsrc = a1_kv[:, 1024:2048].rearrange("p (h d) -> p h d", d=64)
        vph = a1_vph[t % 2].rearrange("p (h d) -> p h d", d=65)[:, :, 0:64]
        vpl = a1_vpl[t % 2].rearrange("p (h d) -> p h d", d=65)[:, :, 0:64]
        v_step(lambda e, vph=vph, vsrc=vsrc: e.tensor_copy(vph, vsrc), inc=False)
        kfvp_mark[t % 2] = v_step(lambda e, vpl=vpl, vsrc=vsrc, vph=vph: e.tensor_tensor(vpl, vsrc, vph, op=OP.subtract))

    tt = NT_ALL - 1
    p.wait("tensor", "sV", kfvp_mark[tt % 2])
    for h in range(NH):
        mm3(vk_bank(h, 64),
            a1_kfh[tt % 2][:, h * 64:(h + 1) * 64],
            a1_kfl[tt % 2][:, h * 64:(h + 1) * 64],
            a1_vph[tt % 2][:, h * 65:(h + 1) * 65],
            a1_vpl[tt % 2][:, h * 65:(h + 1) * 65],
            start=False, stop=True, inc=(h == NH - 1))
    mark_vk[tt] = p.cnt["sT"]

    VKS = a1_kv[0:64, 0:1040]
    p.wait("vector", "sT")
    for g in range(4):
        src = PDD[g // 2][:, (g % 2) * 512:(g % 2) * 512 + 260]
        v_step(lambda e, g=g, src=src: e.tensor_copy(
            a1_kv[0:64, g * 260:(g + 1) * 260], src[0:64, :]), inc=False)
    hi_all = a1_kn[0:64, 0:520].bitcast(BF16)
    lo_all = a1_scr[0:64, 0:520].bitcast(BF16)
    split_v(hi_all, lo_all, VKS, inc=False)
    evens = lambda ap: ap.rearrange("p (h two d) -> p h two d", two=2, d=65)[:, :, 0, :]
    odds = lambda ap: ap.rearrange("p (h two d) -> p h two d", two=2, d=65)[:, :, 1, :]
    v_step(lambda e: e.tensor_copy(evens(VKTh[0:64, :]), evens(hi_all)), inc=False)
    v_step(lambda e: e.tensor_copy(evens(VKTl[0:64, :]), evens(lo_all)))
    p.wait("sync", "sV")
    dma("sync", odds(VKTh[64:128, :]), odds(hi_all), sem="sVD")
    dx_vkt = dma("sync", odds(VKTl[64:128, :]), odds(lo_all), sem="sVD")
    if t_vkt is not None:
        p.wait("sync", "sVD")
        dma("sync", t_vkt[:, 0:520], VKTh[:, :].bitcast(F32), sem="sO")
        dma("sync", t_vkt[:, 520:1040], VKTl[:, :].bitcast(F32), sem="sO")
    if t_kf0 is not None:
        dma("sync", t_kf0[:, 0:512], a1_kfh[tt % 2][:, :].bitcast(F32), sem="sO")
        dma("sync", t_kf0[:, 512:1024], a1_kfl[tt % 2][:, :].bitcast(F32), sem="sO")
    if t_vkt is not None or t_kf0 is not None:
        p.wait("vector", "sO")

    if phases == 1:
        for s in ("sY0", "sY1", "sO"):
            p.on("sync", lambda e, s=s: e.wait_ge(p.sems[s], p.cnt[s]) if p.cnt[s] else None)
        p.replay()
        return nc, tap_t

    p.wait("gpsimd", "sT", mark_vk[NT_ALL - 1])
    p.wait("gpsimd", "sV")
    p.wait("gpsimd", "sVD")
    p.wait("gpsimd", "sWA", wq_out_done)
    p.wait("sync", "sT", mark_vk[NT_ALL - 1])
    p.wait("sync", "sV")
    dx_bq = dma("sync", a2_bq[:], bqkv[0:1, 0:DIM].broadcast_to([128, DIM]))
    dx_gq = dma("sync", a2_gq[:], gqv[0:1, :].broadcast_to([128, DIM]))
    dx_bout = dma("sync", a2_bout[:], boutv[0:1, :].broadcast_to([128, DIM]))
    dx_gout = dma("sync", a2_gout[:], goutv[0:1, :].broadcast_to([128, DIM]))

    p.wait("sync", "sWA", wq_out_done)
    dq_wq = [dma("sync", a2_WQh[:, :], wspq[0], sem="sQ0"),
             dma("sync", a2_WQl[:, :], wspq[1], sem="sQ1"),
             dma("sync", a2_WOh[:, :], wspq[2], sem="sQ2"),
             dma("sync", a2_WOl[:, :], wspq[3], sem="sQ3")]

    sp_state = {"i": SP_START}
    sp_wb = dict(a1_wb_last)
    sp_dk = {}

    def sp_in_dma(ji):
        si, cc = sp_jobs[ji]
        e0, kind = slot_key[si]
        nsub = 512 if kind in ("g", "u") else 1024
        sp_dk[ji] = dma("gpsimd",
                        sp_in[ji % 2].rearrange("p (a n) -> p a n", n=nsub),
                        w_chunk(kind, e0, cc), sem=f"sWS{ji % 2}")

    def emit_split_chunks(nmax):
        for _ in range(nmax):
            if sp_state["i"] >= len(sp_jobs):
                return
            ji = sp_state["i"]
            sp_state["i"] += 1
            si, cc = sp_jobs[ji]
            par = ji % 2
            if ji not in sp_dk:
                sp_in_dma(ji)
            if ji + 1 < len(sp_jobs):
                sp_in_dma(ji + 1)
            p.wait("gpsimd", f"sWS{par}", sp_dk[ji])
            if sp_wb.get(par, 0):
                p.wait("gpsimd", f"sWB{par}", sp_wb[par])
            gmark = split_g(sp_out[par][:, 0:512].bitcast(BF16),
                            sp_out[par][:, 512:1024].bitcast(BF16), sp_in[par])
            p.wait("gpsimd", "sG", gmark)
            sp_wb[par] = dma("gpsimd", wsp[si, :, cc * 2048:(cc + 1) * 2048],
                             sp_out[par][:, :].bitcast(BF16), sem=f"sWB{par}")
            wsp_wbs[si][f"sWB{par}"] = sp_wb[par]

    dx_xin2 = {}
    dx_xin2[0] = dma("sync", a2_xin[0][:], xb[0:128, :], sem="sX0")
    mark_tr2 = {}
    mark_res2 = {}
    mark_x2 = {}
    div_mark = 0

    for i in range(NT_OWN):
        if i + 1 < NT_OWN:
            if i - 1 >= 0:
                p.wait("sync", "sV", mark_res2[i - 1])
            dx_xin2[i + 1] = dma("sync", a2_xin[(i + 1) % 2][:],
                                 xb[(i + 1) * 128:(i + 2) * 128, :],
                                 sem=f"sX{(i + 1) % 2}")
        if i == 0:
            p.wait("tensor", "sX0", dx_xin2[0])
            ts_marks2 = {0: transpose8s(a2_xin[0], a2_TSh, a2_TSl)}
            p.wait("tensor", "sVD", dx_vkt)
            for si_, dv_ in enumerate(dq_wq):
                p.wait("tensor", f"sQ{si_}", dv_)
        ts_mark = ts_marks2[i]
        for oc in range(2):
            bank = PS[2 + oc]
            p.wait("tensor", "sV", max(bank_mark[2 + oc], ts_mark))
            for kc in range(8):
                mm3(bank,
                    a2_TSh[:, kc * 128:(kc + 1) * 128],
                    a2_TSl[:, kc * 128:(kc + 1) * 128],
                    a2_WQh[:, kc * 1024 + oc * 512:kc * 1024 + (oc + 1) * 512],
                    a2_WQl[:, kc * 1024 + oc * 512:kc * 1024 + (oc + 1) * 512],
                    start=(kc == 0), stop=(kc == 7), inc=(kc == 7))
        p.wait("vector", "sT")
        if i == 0:
            p.wait("vector", "sP")
        for oc in range(2):
            bank_mark[2 + oc] = v_step(lambda e, oc=oc: e.tensor_tensor(
                a2_b1[:, oc * 512:(oc + 1) * 512], PS[2 + oc],
                a2_bq[:, oc * 512:(oc + 1) * 512], op=OP.add), inc=True)
        p.wait("scalar", "sV")
        s_step(lambda e: e.activation(a2_scr, a2_b1, AF.Square, accum_out=ssq), inc=False)
        s_step(lambda e: e.activation(st1, ssq, AF.Ln, scale=1.0 / DIM, bias=c_rmseps), inc=False)
        s_step(lambda e: e.activation(rstd, st1, AF.Exp, scale=-0.5))
        p.wait("vector", "sS")
        v_step(lambda e: e.scalar_tensor_tensor(a2_b2, a2_b1, rstd, a2_gq,
                                                op0=OP.mult, op1=OP.mult), inc=False)
        cs = cos_tab[:, i * 32:(i + 1) * 32].unsqueeze(1).broadcast_to([128, 16, 32])
        sn = sin_tab[:, i * 32:(i + 1) * 32].unsqueeze(1).broadcast_to([128, 16, 32])
        qnv = a2_b2[:].rearrange("p (h d two) -> p h d two", h=16, two=2)
        qfv = a2_b3[:].rearrange("p (h d two) -> p h d two", h=16, two=2)
        tr, ti = qnv[:, :, :, 0], qnv[:, :, :, 1]
        sc1 = a2_scr[:, 0:512].rearrange("p (h d) -> p h d", d=32)
        sc2 = a2_scr[:, 512:1024].rearrange("p (h d) -> p h d", d=32)
        v_step(lambda e, tr=tr, cs=cs, sc1=sc1: e.tensor_tensor(sc1, tr, cs, op=OP.mult), inc=False)
        v_step(lambda e, ti=ti, sn=sn, sc2=sc2: e.tensor_tensor(sc2, ti, sn, op=OP.mult), inc=False)
        v_step(lambda e, qfv=qfv, sc1=sc1, sc2=sc2: e.tensor_tensor(qfv[:, :, :, 0], sc1, sc2, op=OP.subtract), inc=False)
        v_step(lambda e, tr=tr, sn=sn, sc1=sc1: e.tensor_tensor(sc1, tr, sn, op=OP.mult), inc=False)
        v_step(lambda e, ti=ti, cs=cs, sc2=sc2: e.tensor_tensor(sc2, ti, cs, op=OP.mult), inc=False)
        v_step(lambda e, qfv=qfv, sc1=sc1, sc2=sc2: e.tensor_tensor(qfv[:, :, :, 1], sc1, sc2, op=OP.add), inc=False)
        qf_mark = v_step(lambda e: e.tensor_scalar(a2_b3[:], a2_b3[:], 0.0, None, op0=OP.max))
        if i == 0 and t_qf0 is not None:
            p.wait("sync", "sV")
            dma("sync", t_qf0[:], a2_b3[:], sem="sO")
            p.wait("vector", "sO")
        ts_mark = transpose8s(a2_b3, a2_TSh, a2_TSl, src_mark=qf_mark)
        p.wait("tensor", "sV", max(div_mark, ts_mark))
        for h in range(NH):
            mm3(vk_bank(h),
                a2_TSh[:, 128 * (h // 2):128 * (h // 2) + 128],
                a2_TSl[:, 128 * (h // 2):128 * (h // 2) + 128],
                VKTh[:, 65 * h:65 * h + 65],
                VKTl[:, 65 * h:65 * h + 65],
                start=True, stop=True, inc=(h == NH - 1))
        p.wait("vector", "sT")
        for g in range(4):
            src = PDD[g // 2][:, (g % 2) * 512:(g % 2) * 512 + 512]
            v_step(lambda e, g=g, src=src: e.tensor_scalar(
                dnm[:, g * 4:(g + 1) * 4],
                src[:, 0:260].rearrange("p (h d) -> p h d", d=65)[:, :, 64],
                EPS_LIN, None, op0=OP.add), inc=False)
        v_step(lambda e: e.reciprocal(rcp, dnm), inc=False)
        for g in range(4):
            src = PDD[g // 2][:, (g % 2) * 512:(g % 2) * 512 + 512]
            v_step(lambda e, g=g, src=src: e.tensor_tensor(
                a2_b4[:, g * 256:(g + 1) * 256].rearrange("p (h d) -> p h d", d=64),
                src[:, 0:260].rearrange("p (h d) -> p h d", d=65)[:, :, 0:64],
                rcp[:, g * 4:(g + 1) * 4].unsqueeze(2).broadcast_to([128, 4, 64]),
                op=OP.mult), inc=(g == 3))
        div_mark = p.cnt["sV"]
        if i == 0 and t_attn0 is not None:
            p.wait("sync", "sV")
            dma("sync", t_attn0[:], a2_b4[:], sem="sO")
            p.wait("vector", "sO")
        ts_mark = transpose8s(a2_b4, a2_TSh, a2_TSl, src_mark=div_mark)
        for oc in range(2):
            bank = PS[2 + oc]
            p.wait("tensor", "sV", max(bank_mark[2 + oc], ts_mark))
            for kc in range(8):
                mm3(bank,
                    a2_TSh[:, kc * 128:(kc + 1) * 128],
                    a2_TSl[:, kc * 128:(kc + 1) * 128],
                    a2_WOh[:, kc * 1024 + oc * 512:kc * 1024 + (oc + 1) * 512],
                    a2_WOl[:, kc * 1024 + oc * 512:kc * 1024 + (oc + 1) * 512],
                    start=(kc == 0), stop=(kc == 7), inc=(kc == 7))
        out_mm_mark = p.cnt["sT"]
        p.wait("vector", "sT", out_mm_mark)
        for oc in range(2):
            bank_mark[2 + oc] = v_step(lambda e, oc=oc: e.tensor_tensor(
                a2_b1[:, oc * 512:(oc + 1) * 512], PS[2 + oc],
                a2_bout[:, oc * 512:(oc + 1) * 512], op=OP.add), inc=True)
        out_evac_mark = p.cnt["sV"]
        if i + 1 < NT_OWN:
            p.wait("tensor", f"sX{(i + 1) % 2}", dx_xin2[i + 1])
            ts_marks2[i + 1] = transpose8s(a2_xin[(i + 1) % 2], a2_TSh, a2_TSl)
        p.wait("scalar", "sV", out_evac_mark)
        s_step(lambda e: e.activation(a2_scr, a2_b1, AF.Square, accum_out=ssq), inc=False)
        s_step(lambda e: e.activation(st1, ssq, AF.Ln, scale=1.0 / DIM, bias=c_rmseps), inc=False)
        s_step(lambda e: e.activation(rstd, st1, AF.Exp, scale=-0.5))
        p.wait("vector", "sS")
        v_step(lambda e: e.scalar_tensor_tensor(a2_b2, a2_b1, rstd, a2_gout,
                                                op0=OP.mult, op1=OP.mult), inc=False)
        v_step(lambda e, i=i: e.tensor_tensor(a2_b3[:], a2_b2, a2_xin[i % 2][:], op=OP.add))
        mark_res2[i] = p.cnt["sV"]
        p.wait("scalar", "sV")
        s_step(lambda e: e.activation(a2_scr, a2_b3[:], AF.Copy, accum_out=ssum), inc=False)
        s_step(lambda e: e.activation(a2_scr, a2_b3[:], AF.Square, accum_out=ssq))
        p.wait("vector", "sS")
        v_step(lambda e: e.tensor_scalar(smean, ssum, 1.0 / DIM, None, op0=OP.mult), inc=False)
        v_step(lambda e: e.tensor_tensor(st1, smean, smean, op=OP.mult), inc=False)
        v_step(lambda e: e.scalar_tensor_tensor(svar, ssq, 1.0 / DIM, st1,
                                                op0=OP.mult, op1=OP.subtract))
        p.wait("scalar", "sV")
        s_step(lambda e: e.activation(st1, svar, AF.Ln, bias=c_lneps), inc=False)
        s_step(lambda e: e.activation(rstd, st1, AF.Exp, scale=-0.5))
        p.wait("vector", "sS")
        mark_x2[i] = v_step(lambda e, i=i: e.tensor_scalar(X2[i], a2_b3[:], smean, rstd,
                                              op0=OP.subtract, op1=OP.mult))
        if phases == 3:
            emit_split_chunks(8)

    if phases == 3:
        emit_split_chunks(len(sp_jobs))

    if t_x2 is not None:
        p.wait("sync", "sV")
        for i in range(NT_OWN):
            dma("sync", t_x2[i * 128:(i + 1) * 128, :], X2[i], sem="sO")
        p.wait("vector", "sO")

    if phases == 2:
        for s in ("sY0", "sY1", "sO"):
            p.on("sync", lambda e, s=s: e.wait_ge(p.sems[s], p.cnt[s]) if p.cnt[s] else None)
        p.replay()
        return nc, tap_t

    p.wait("gpsimd", "sT")
    p.wait("gpsimd", "sV")
    dx_gw = dma("gpsimd", c_gwf[:].rearrange("p (c n) -> p c n", n=8),
                gatew[:].rearrange("(c p) n -> p c n", p=128), sem="sPG")
    dx_gateb = dma("gpsimd", c_gateb[:], gateb[0:1, :].broadcast_to([128, 8]), sem="sPG")
    p.wait("vector", "sPG")
    p.wait("vector", "sWB0")
    p.wait("vector", "sWB1")
    p.wait("scalar", "sWB0")
    p.wait("scalar", "sWB1")
    gw_split_v = split_v(c_gwh[:, :], c_gwl[:, :], c_gwf[:])

    slot_mark = [0, 0, 0, 0]
    load_val = {}
    load_q = [(qq, si) for qq in range(NQ) for si in range(NSLOT)]
    slot_of = {}
    qstate = {"qi": 0}
    ring_started = {"done": False}

    def emit_next_load(s):
        if qstate["qi"] >= len(load_q):
            return
        key = load_q[qstate["qi"]]
        qstate["qi"] += 1
        p.wait("gpsimd", "sT", slot_mark[s])
        _, si = key
        for semn, val in wsp_wbs[si].items():
            p.wait("gpsimd", semn, val)
        load_val[key] = dma("gpsimd", c_ring[s][:, :], wsp[si], sem=f"sW{s}")
        slot_of[key] = s

    for s in range(4):
        emit_next_load(s)

    def free_slot(qq, si):
        s = slot_of[(qq, si)]
        slot_mark[s] = p.cnt["sT"]
        emit_next_load(s)

    def ring_views(slot, kind):
        sb = c_ring[slot]
        if kind in ("g", "u"):
            def wh(c):
                cc, off = divmod(c, 2)
                return sb[:, cc * 2048 + off * 512:cc * 2048 + off * 512 + 512]
            def wl(c):
                cc, off = divmod(c, 2)
                return sb[:, cc * 2048 + 1024 + off * 512:cc * 2048 + 1024 + off * 512 + 512]
        else:
            def wh(c, oc):
                return sb[:, c * 2048 + oc * 512:c * 2048 + oc * 512 + 512]
            def wl(c, oc):
                return sb[:, c * 2048 + 1024 + oc * 512:c * 2048 + 1024 + oc * 512 + 512]
        return wh, wl

    out_dma = {}
    s_markPS = {2: 0, 3: 0}
    v_markPS = {2: 0, 3: 0}
    yq_mark = {0: 0, 1: 0}
    gu_mark = {0: 0, 1: 0}
    def prep_quarter(q):
        tiles = [q * NT_Q + j for j in range(NT_Q)]
        for j, ti_ in enumerate(tiles):
            xtq_mark = transpose8s(X2[ti_], c_xTqh[:, j * 1024:(j + 1) * 1024],
                                   c_xTql[:, j * 1024:(j + 1) * 1024],
                                   src_mark=mark_x2.get(ti_, 0))
        p.wait("tensor", "sPG")
        p.wait("tensor", "sV", xtq_mark)
        for j in range(NT_Q):
            gb = 2 + j % 2
            p.wait("tensor", "sS", s_markPS[gb])
            p.wait("tensor", "sV", v_markPS[gb])
            for kc in range(8):
                mm3(PS[gb][:, 0:8],
                    c_xTqh[:, j * 1024 + kc * 128:j * 1024 + (kc + 1) * 128],
                    c_xTql[:, j * 1024 + kc * 128:j * 1024 + (kc + 1) * 128],
                    c_gwh[:, kc * 8:(kc + 1) * 8],
                    c_gwl[:, kc * 8:(kc + 1) * 8],
                    start=(kc == 0), stop=(kc == 7), inc=(kc == 7))
            p.wait("scalar", "sT")
            p.wait("scalar", "sV")
            s_markPS[gb] = s_step(lambda e, gb=gb: e.activation(c_e, PS[gb][:, 0:8], AF.Exp, scale=-1.0))
            p.wait("vector", "sS")
            v_step(lambda e: e.tensor_scalar(c_e, c_e, 1.0, None, op0=OP.add), inc=False)
            v_step(lambda e: e.reciprocal(c_s, c_e), inc=False)
            v_step(lambda e: e.tensor_tensor(c_s, c_s, c_gateb, op=OP.add), inc=False)
            sv = c_s.rearrange("p (g t) -> p g t", t=2)
            v_step(lambda e, sv=sv: e.tensor_tensor(c_G, sv[:, :, 0], sv[:, :, 1], op=OP.add), inc=False)
            gj = c_G.unsqueeze(1).broadcast_to([128, 4, 4])
            gi = c_G.unsqueeze(2).broadcast_to([128, 4, 4])
            v_step(lambda e, gj=gj, gi=gi: e.tensor_tensor(c_scr1[:, 0:16], gj, gi, op=OP.is_gt), inc=False)
            v_step(lambda e, gj=gj, gi=gi: e.tensor_tensor(c_scr2[:, 0:16], gj, gi, op=OP.is_equal), inc=False)
            v_step(lambda e: e.tensor_tensor(c_scr2[:, 0:16], c_scr2[:, 0:16], emask4, op=OP.mult), inc=False)
            v_step(lambda e: e.tensor_tensor(c_scr1[:, 0:16], c_scr1[:, 0:16], c_scr2[:, 0:16], op=OP.add), inc=False)
            v_step(lambda e: e.tensor_reduce(c_rank[:, 0:4], c_scr1[:, 0:16].rearrange("p (i j) -> p i j", j=4), axis=AX.X, op=OP.add), inc=False)
            v_step(lambda e: e.tensor_scalar(c_rank[:, 0:4], c_rank[:, 0:4], 1.5, None, op0=OP.is_lt), inc=False)
            gm = c_rank[:, 0:4].unsqueeze(2).broadcast_to([128, 4, 2])
            v_step(lambda e, gm=gm: e.tensor_tensor(c_m8.rearrange("p (g t) -> p g t", t=2), c_s.rearrange("p (g t) -> p g t", t=2), gm, op=OP.mult), inc=False)
            mj = c_m8.unsqueeze(1).broadcast_to([128, 8, 8])
            mi = c_m8.unsqueeze(2).broadcast_to([128, 8, 8])
            v_step(lambda e, mj=mj, mi=mi: e.tensor_tensor(c_scr1, mj, mi, op=OP.is_gt), inc=False)
            v_step(lambda e, mj=mj, mi=mi: e.tensor_tensor(c_scr2, mj, mi, op=OP.is_equal), inc=False)
            v_step(lambda e: e.tensor_tensor(c_scr2, c_scr2, emask8, op=OP.mult), inc=False)
            v_step(lambda e: e.tensor_tensor(c_scr1, c_scr1, c_scr2, op=OP.add), inc=False)
            v_step(lambda e: e.tensor_reduce(c_rank, c_scr1.rearrange("p (i j) -> p i j", j=8), axis=AX.X, op=OP.add), inc=False)
            v_step(lambda e: e.tensor_scalar(c_sel, c_rank, 1.5, None, op0=OP.is_lt), inc=False)
            v_step(lambda e: e.tensor_tensor(c_sw, c_sel, c_s, op=OP.mult), inc=False)
            v_step(lambda e: e.tensor_reduce(wsum, c_sw, axis=AX.X, op=OP.add), inc=False)
            v_step(lambda e: e.tensor_scalar(wsum, wsum, 1e-20, None, op0=OP.add), inc=False)
            v_step(lambda e: e.reciprocal(wnorm, wsum), inc=False)
            v_step(lambda e: e.tensor_scalar(wnorm, wnorm, ROUTED_SCALE, None, op0=OP.mult), inc=False)
            v_step(lambda e, j=j: e.tensor_scalar(WD[:, j * 8:(j + 1) * 8], c_sw, wnorm, None, op0=OP.mult))
        if t_wd is not None:
            p.wait("sync", "sV")
            dma("sync", t_wd[:, q * 32:(q + 1) * 32], WD[:], sem="sO")
            p.wait("vector", "sO")


    prep_quarter(0)
    for q in range(NQ):
        tiles = [q * NT_Q + j for j in range(NT_Q)]
        for e_i in range(NE):
            routed = e_i < 8
            si_g = e_i * 3
            si_u = e_i * 3 + 1
            si_d = e_i * 3 + 2
            gwh_, gwl_ = ring_views(slot_of[(q, si_g)], "g")
            uwh_, uwl_ = ring_views(slot_of[(q, si_u)], "u")
            dwh_, dwl_ = ring_views(slot_of[(q, si_d)], "d")
            p.wait("tensor", f"sW{slot_of[(q, si_g)]}", load_val[(q, si_g)])
            for j in range(NT_Q):
                gb = 2 + j % 2
                p.wait("tensor", "sS", s_markPS[gb])
                p.wait("tensor", "sV", v_markPS[gb])
                for kc in range(8):
                    mm3(PS[gb],
                        c_xTqh[:, j * 1024 + kc * 128:j * 1024 + (kc + 1) * 128],
                        c_xTql[:, j * 1024 + kc * 128:j * 1024 + (kc + 1) * 128],
                        gwh_(kc), gwl_(kc),
                        start=(kc == 0), stop=(kc == 7), inc=(kc == 7))
                p.wait("scalar", "sT")
                p.wait("scalar", "sV", gu_mark[j % 2])
                s_markPS[gb] = s_step(lambda e, j=j, gb=gb: e.activation(c_g[j], PS[gb], AF.Silu))
            free_slot(q, si_g)
            p.wait("tensor", f"sW{slot_of[(q, si_u)]}", load_val[(q, si_u)])
            for j in range(NT_Q):
                gb = 2 + j % 2
                p.wait("tensor", "sV", v_markPS[gb])
                p.wait("tensor", "sS", s_markPS[gb])
                for kc in range(8):
                    mm3(PS[gb],
                        c_xTqh[:, j * 1024 + kc * 128:j * 1024 + (kc + 1) * 128],
                        c_xTql[:, j * 1024 + kc * 128:j * 1024 + (kc + 1) * 128],
                        uwh_(kc), uwl_(kc),
                        start=(kc == 0), stop=(kc == 7), inc=(kc == 7))
                p.wait("vector", "sT")
                p.wait("vector", "sS")
                if routed:
                    m_ = v_step(lambda e, j=j, e_i=e_i, gb=gb: e.scalar_tensor_tensor(
                        c_gu[j], PS[gb], WD[:, j * 8 + e_i:j * 8 + e_i + 1],
                        c_g[j], op0=OP.mult, op1=OP.mult))
                else:
                    m_ = v_step(lambda e, j=j, gb=gb: e.tensor_tensor(c_gu[j], PS[gb], c_g[j], op=OP.mult))
                v_markPS[gb] = m_
                gu_mark[j % 2] = m_
            free_slot(q, si_u)
            p.wait("tensor", f"sW{slot_of[(q, si_d)]}", load_val[(q, si_d)])
            for j in range(NT_Q):
                db = j % 2
                p.wait("tensor", "sV", max(marks["ps01"], gu_mark[j % 2]))
                for kc in range(4):
                    t_step(lambda e, j=j, kc=kc, db=db: e.transpose(
                        PS[db][:, kc * 128:(kc + 1) * 128],
                        c_gu[j][:, kc * 128:(kc + 1) * 128], ident),
                        inc=(kc == 3))
                p.wait("vector", "sT")
                guts = split_v(c_guTh[db][:, :], c_guTl[db][:, :], PS[db])
                marks["ps01"] = guts
                p.wait("tensor", "sV", max(guts, yq_mark[db]))
                for oc in range(2):
                    for kc in range(4):
                        mm3(PDD[db][:, oc * 512:(oc + 1) * 512],
                            c_guTh[db][:, kc * 128:(kc + 1) * 128],
                            c_guTl[db][:, kc * 128:(kc + 1) * 128],
                            dwh_(kc, oc), dwl_(kc, oc),
                            start=(kc == 0), stop=(kc == 3),
                            inc=(oc == 1 and kc == 3))
                p.wait("vector", "sT")
                if e_i == 0:
                    yq_mark[db] = v_step(lambda e, j=j, db=db: e.tensor_copy(c_Yq[j], PDD[db][:, 0:1024]))
                else:
                    yq_mark[db] = v_step(lambda e, j=j, db=db: e.tensor_tensor(c_Yq[j], c_Yq[j], PDD[db][:, 0:1024], op=OP.add))
            free_slot(q, si_d)


        if q == 0 and t_y0 is not None:
            p.wait("sync", "sV")
            dma("sync", t_y0[:], c_Yq[0], sem="sO")
            p.wait("vector", "sO")

        if q + 1 < NQ:
            prep_quarter(q + 1)

        for j, ti_ in enumerate(tiles):
            ob = c_out[j % 2]
            if q * NT_Q + j >= 2:
                p.wait("vector", f"sY{j % 2}", out_dma.get(q * NT_Q + j - 2, 0))
            v_step(lambda e, j=j, ti_=ti_, ob=ob: e.tensor_tensor(ob[:], c_Yq[j], X2[ti_], op=OP.add))
            p.wait("scalar", "sV")
            s_step(lambda e, ob=ob: e.activation(c_scrA, ob[:], AF.Copy, accum_out=ssum), inc=False)
            s_step(lambda e, ob=ob: e.activation(c_scrA, ob[:], AF.Square, accum_out=ssq))
            p.wait("vector", "sS")
            v_step(lambda e: e.tensor_scalar(smean, ssum, 1.0 / DIM, None, op0=OP.mult), inc=False)
            v_step(lambda e: e.tensor_tensor(st1, smean, smean, op=OP.mult), inc=False)
            v_step(lambda e: e.scalar_tensor_tensor(svar, ssq, 1.0 / DIM, st1,
                                                    op0=OP.mult, op1=OP.subtract))
            p.wait("scalar", "sV")
            s_step(lambda e: e.activation(st1, svar, AF.Ln, bias=c_lneps), inc=False)
            s_step(lambda e: e.activation(rstd, st1, AF.Exp, scale=-0.5))
            p.wait("vector", "sS")
            v_step(lambda e, ob=ob: e.tensor_scalar(ob[:], ob[:], smean, rstd,
                                                    op0=OP.subtract, op1=OP.mult))
            p.wait("sync", "sV")
            out_dma[q * NT_Q + j] = dma("sync", y[ti_ * 128:(ti_ + 1) * 128, :],
                                        ob[:], sem=f"sY{j % 2}")

    for s in ("sY0", "sY1", "sO"):
        p.on("sync", lambda e, s=s: e.wait_ge(p.sems[s], p.cnt[s]) if p.cnt[s] else None)

    p.replay()
    return nc, tap_t


def kernel(**inputs):
    x = np.asarray(inputs["L0_attn_x"])
    Wqkv = np.asarray(inputs["L0_attn_Wqkv"])
    bqkv = np.asarray(inputs["L0_attn_bqkv"])
    gq = np.asarray(inputs["L0_attn_gq"])
    gk = np.asarray(inputs["L0_attn_gk"])
    Wout = np.asarray(inputs["L0_attn_Wout"])
    bout = np.asarray(inputs["L0_attn_bout"])
    gout = np.asarray(inputs["L0_attn_gout"])
    gate_w = np.asarray(inputs["L0_ffn_gate_w"])
    gate_b = np.asarray(inputs["L0_ffn_gate_bias"])
    w_gate = np.asarray(inputs["L0_ffn_w_gate"])
    w_up = np.asarray(inputs["L0_ffn_w_up"])
    w_down = np.asarray(inputs["L0_ffn_w_down"])
    sw_gate = np.asarray(inputs["L0_ffn_sw_gate"])
    sw_up = np.asarray(inputs["L0_ffn_sw_up"])
    sw_down = np.asarray(inputs["L0_ffn_sw_down"])

    nc, _ = build_program()
    C = np.ascontiguousarray
    shared = {
        "wkvT": C(Wqkv[DIM:3 * DIM].T),
        "wqT": C(Wqkv[0:DIM].T),
        "woutT": C(Wout.T),
        "bqkv": C(bqkv.reshape(1, 3 * DIM)),
        "gqv": C(gq.reshape(1, DIM)),
        "gkv": C(gk.reshape(1, DIM)),
        "goutv": C(gout.reshape(1, DIM)),
        "boutv": C(bout.reshape(1, DIM)),
        "gatew": C(gate_w),
        "gateb": C(gate_b.reshape(1, 8)),
        "wg": C(w_gate),
        "wu": C(w_up),
        "wd": C(w_down),
        "swg": C(sw_gate),
        "swu": C(sw_up),
        "swd": C(sw_down),
    }
    in_maps = []
    for c in range(8):
        b, h = divmod(c, 2)
        own = x[b, h * 2048:(h + 1) * 2048]
        fore = x[b, (1 - h) * 2048:(2 - h) * 2048]
        xbc = np.concatenate([own, fore], axis=0)
        poff = np.zeros((1, NT_ALL), dtype=np.float32)
        poff[0, :16] = h * 2048
        poff[0, 16:] = (1 - h) * 2048 - 2048
        m = dict(shared)
        m["xb"] = C(xbc)
        m["posoff"] = poff
        in_maps.append(m)

    res = run_bass_kernel_spmd(nc, in_maps, core_ids=list(range(8)))
    out = np.empty((4, 4096, DIM), dtype=np.float32)
    for c in range(8):
        b, h = divmod(c, 2)
        out[b, h * 2048:(h + 1) * 2048] = res.results[c]["y"]
    return out
